# version 01
# baseline (speedup 1.0000x reference)
import contextlib
import numpy as np
import ml_dtypes
import concourse.bass as bass
import concourse.mybir as mybir
from concourse.bass_utils import run_bass_kernel_spmd

F32 = mybir.dt.float32
BF16 = mybir.dt.bfloat16
AF = mybir.ActivationFunctionType
ALU = mybir.AluOpType
AX = mybir.AxisListType

RND = float(3 << 22)
EPS = 1e-5
QMAX = 127.0

D = 1024
DI = 4096
I = 2816
DC = D // 128
DIC = DI // 128
IC = I // 128

ENGINES = ("tensor", "vector", "scalar", "gpsimd", "sync")
Q_LOAD = "sync"
Q_STORE = "scalar"
Q_CAST = "gpsimd"
RG8 = [[0, 1, 2, 3, 4, 5, 6, 7]]


class Ctx:
    def __init__(self, nc, es):
        self.nc = nc
        self.es = es
        self.ops = {e: [] for e in ENGINES}
        self.sems = {}
        self.cnt = {}
        for e in ENGINES:
            self.new_sem("p_" + e)

    def new_sem(self, name):
        assert name not in self.sems
        self.sems[name] = self.es.enter_context(self.nc.semaphore(name))
        self.cnt[name] = 0
        return name

    def emit(self, eng, fn):
        self.ops[eng].append(fn)

    SERIAL = ("vector", "scalar", "gpsimd")

    def op(self, eng, fn, inc=True):
        name = "p_" + eng
        if eng in self.SERIAL:
            cur = self.cnt[name]
            if cur > 0:
                self.emit(eng, lambda e, s=self.sems[name], t=cur: e.wait_ge(s, t))
            self.cnt[name] += 1
            tgt = self.cnt[name]
            s = self.sems[name]
            self.emit(eng, lambda e, fn=fn, s=s: fn(e).then_inc(s, 1))
            return tgt
        if inc:
            self.cnt[name] += 1
            tgt = self.cnt[name]
            s = self.sems[name]
            self.emit(eng, lambda e, fn=fn, s=s: fn(e).then_inc(s, 1))
            return tgt
        self.emit(eng, lambda e, fn=fn: fn(e))
        return None

    def op_free(self, eng, fn, free_sem):
        self.cnt[free_sem] += 1
        s = self.sems[free_sem]
        self.emit(eng, lambda e, fn=fn, s=s: fn(e).then_inc(s, 1))
        return self.cnt[free_sem]

    def wait(self, eng, sem, tgt):
        if tgt is None or tgt <= 0:
            return
        self.emit(eng, lambda e, s=self.sems[sem], t=tgt: e.wait_ge(s, t))

    def wp(self, eng, other, tgt):
        self.wait(eng, "p_" + other, tgt)

    def dma(self, q, out_ap, in_ap, sem, slow=False):
        self.cnt[sem] += 16
        tgt = self.cnt[sem]
        s = self.sems[sem]
        self.emit(q, lambda e, o=out_ap, i=in_ap, s=s, sl=slow: e.dma_start(
            out=o, in_=i, allow_slow_non_contiguous=sl).then_inc(s, 16))
        return tgt

    def qjoin(self, q):
        for sname, cval in list(self.cnt.items()):
            if sname.startswith("d_" + q) and cval > 0:
                self.wait(q, sname, cval)
        name = "p_" + q
        self.cnt[name] += 1
        tgt = self.cnt[name]
        s = self.sems[name]
        self.emit(q, lambda e, s=s: e.sem_inc(s, 1))
        return tgt

    def barrier(self):
        tgts = {}
        for e in ENGINES:
            for sname, cval in list(self.cnt.items()):
                if sname.startswith("d_" + e) and cval > 0:
                    self.wait(e, sname, cval)
            name = "p_" + e
            self.cnt[name] += 1
            tgts[e] = self.cnt[name]
            s = self.sems[name]
            def f(eng, s=s):
                eng.drain()
                eng.sem_inc(s, 1)
            self.emit(e, f)
        for e in ENGINES:
            for o in ENGINES:
                if o != e:
                    self.wp(e, o, tgts[o])


class Stream:

    def __init__(self, ctx, name, q, nslots):
        self.c = ctx
        self.q = q
        self.n = nslots
        self.sl = [ctx.new_sem(f"d_{q}_{name}{j}") for j in range(nslots)]
        self.i = 0

    def fill(self, out_ap, in_ap, free=None):
        c = self.c
        if free is not None:
            sem, tgt = free
            c.wait(self.q, sem, tgt)
        s = self.sl[self.i % self.n]
        tgt = c.dma(self.q, out_ap, in_ap, s)
        self.i += 1
        return s, tgt


class StoreSlots:

    def __init__(self, ctx, name, q, nslots):
        self.c = ctx
        self.q = q
        self.n = nslots
        self.sl = [ctx.new_sem(f"d_{q}_{name}{j}") for j in range(nslots)]
        self.i = 0

    def store(self, out_ap, in_ap, ready_eng, ready_tgt):
        c = self.c
        c.wp(self.q, ready_eng, ready_tgt)
        s = self.sl[self.i % self.n]
        tgt = c.dma(self.q, out_ap, in_ap, s)
        self.i += 1
        return s, tgt


def build_program(T=2048, taps=(), stop=99):
    TC = T // 128
    TH = T // 2
    NTS = TH // 512
    NTT = T // 512
    ZS = T // 512

    nc = bass.Bass("TRN2", target_bir_lowering=False, debug=False)

    def din(name, shape):
        return nc.dram_tensor(name, shape, F32, kind="ExternalInput").ap()

    x_h = din("x", [T, D])
    lng_h = din("ln_g", [D]); lnb_h = din("ln_b", [D])
    h1w1_h = din("h1w1", [D, DI]); h1b1_h = din("h1b1", [DI])
    h1w2_h = din("h1w2", [DI, D]); h1b2_h = din("h1b2", [D])
    h2w1_h = din("h2w1", [D, DI]); h2b1_h = din("h2b1", [DI])
    h2w2_h = din("h2w2", [DI, D]); h2b2_h = din("h2b2", [D])
    fmw1_h = din("fmw1", [D, DI]); fmb1_h = din("fmb1", [DI])
    fmw2_h = din("fmw2", [DI, D]); fmb2_h = din("fmb2", [D])
    clipx_h = din("clip_x", [1]); clipd_h = din("clip_down", [1])
    wg_h = din("w_gate", [I, D]); wu_h = din("w_up", [I, D]); wd_h = din("w_down", [D, I])
    out_h = nc.dram_tensor("out", [T, D], F32, kind="ExternalOutput").ap()

    def dscr(name, shape, dt=BF16):
        return nc.dram_tensor(name, shape, dt).ap()

    ntm_d = dscr("ntm_d", [T, D])
    w1_d = dscr("w1_d", [DC, 128, T])
    w2_d = dscr("w2_d", [DC, 128, T])
    r_d = dscr("r_d", [T, D], F32)
    xmid_d = dscr("xmid_d", [T, D], F32)
    hpre_d = dscr("hpre_d", [IC, 128, T])
    qwg_d = dscr("qwg_d", [IC, DC, 128, 128])
    qwu_d = dscr("qwu_d", [IC, DC, 128, 128])
    sdrow_d = dscr("sdrow_d", [DC, 128], F32)
    scr8_d = dscr("scr8_d", [8], F32)
    cc1i = nc.dram_tensor("cc1i", [128, 8], F32)
    cc1o = nc.dram_tensor("cc1o", [128, 8], F32)
    cc2i = nc.dram_tensor("cc2i", [128, 8], F32)
    cc2o = nc.dram_tensor("cc2o", [128, 8], F32)

    tap_outs = {}
    for t in taps:
        shp, dt = {
            "ntm": ([T, D], BF16), "nfm": ([DC * 128, T], BF16),
            "w1": ([DC, 128, T], BF16), "w2": ([DC, 128, T], BF16),
            "gt": ([T, D], BF16), "r": ([T, D], F32), "xmid": ([T, D], F32),
            "mfm": ([DC * 128, T], BF16), "xq": ([DC * 128, T], BF16),
            "hpre": ([IC, 128, T], BF16), "qwdT": ([IC * 128, D], BF16),
        }[t]
        tap_outs[t] = nc.dram_tensor("tap_" + t, shp, dt, kind="ExternalOutput").ap()

    class _Stop(Exception):
        pass

    es = contextlib.ExitStack()
    with es:
        sb = lambda name, shape, dt: es.enter_context(nc.sbuf_tensor(name, shape, dt))
        AWIDE = max(16 * T, 32768)
        AR1 = sb("AR1", [128, AWIDE], BF16)
        AR2 = sb("AR2", [128, AWIDE], BF16)
        AR3 = sb("AR3", [128, 8 * T], BF16)
        CH = sb("CH", [128, 4 * D], F32)
        WS = sb("WS", [128, 2, 8, 256], BF16)
        g_bc = sb("g_bc", [128, D], F32)
        b_bc = sb("b_bc", [128, D], F32)
        sdh_bc = sb("sdh_bc", [128, D], F32)
        ident_b = sb("ident_b", [128, 128], BF16)
        ident_f = sb("ident_f", [128, 128], F32)
        trimask = sb("trimask", [128, 128], BF16)
        ones_b = sb("ones_b", [128, 128], BF16)
        onesrow_b = sb("onesrow_b", [1, 128], BF16)
        st6 = sb("st6", [128, 12], F32)
        mv2 = sb("mv2", [128, 2], F32)
        tmp1 = sb("tmp1", [128, 1], F32)
        tmp2 = sb("tmp2", [128, 1], F32)
        tmp3 = sb("tmp3", [128, 1], F32)
        tmpr = sb("tmpr", [128, 1], F32)
        mu_all = sb("mu_all", [128, TC], F32)
        rsx_all = sb("rsx_all", [128, TC], F32)
        nmrs_all = sb("nmrs_all", [128, TC], F32)
        rsc_all = sb("rsc_all", [128, TC], F32)
        nmrsc_all = sb("nmrsc_all", [128, TC], F32)
        mu1_all = sb("mu1_all", [128, TC], F32)
        rs1_all = sb("rs1_all", [128, TC], F32)
        nmrs1_all = sb("nmrs1_all", [128, TC], F32)
        bvec32 = sb("bvec32", [128, 32], F32)
        bvec8 = sb("bvec8", [128, 8], F32)
        brow = sb("brow", [32, 128], F32)
        fmb2row = sb("fmb2row", [1, D], BF16)
        colmax = sb("colmax", [128, TC], F32)
        hmax_all = sb("hmax_all", [128, IC], F32)
        cl8 = sb("cl8", [128, 8], F32)
        ggrow = sb("ggrow", [1, 128], F32)
        gsc = sb("gsc", [1, 8], F32)
        clipsb = sb("clipsb", [1, 2], F32)
        sx_vec = sb("sx_vec", [128, 1], F32)
        rsx_q = sb("rsx_q", [128, 1], F32)
        sh_vec = sb("sh_vec", [128, 1], F32)
        rsh_q = sb("rsh_q", [128, 1], F32)
        sg_all = sb("sg_all", [128, IC], F32)
        su_all = sb("su_all", [128, IC], F32)
        sd_all = sb("sd_all", [128, DC], F32)
        nchunk = sb("nchunk", [128, D], BF16)
        evb = sb("evb", [128, 2, 512], BF16)

        PS_A = es.enter_context(nc.psum_tensor("PS_A", [128, 2048], F32))
        PS_B = es.enter_context(nc.psum_tensor("PS_B", [128, 1536], F32))
        PS_TR = es.enter_context(nc.psum_tensor("PS_TR", [128, 8, 128], BF16))

        c = Ctx(nc, es)
        def ckstop(n):
            if stop <= n:
                raise _Stop()
        dS = c.new_sem("d_sync_g")
        dG = c.new_sem("d_gpsimd_g")
        dDQ = c.new_sem("d_sync_dq")

        xs = Stream(c, "xs", Q_LOAD, 2)
        ws = Stream(c, "ws", Q_CAST, 2)
        wrs = Stream(c, "wrs", Q_CAST, 2)
        qws = Stream(c, "qws", Q_LOAD, 4)
        ntm_st = StoreSlots(c, "ntm", Q_STORE, 1)
        wout_st = StoreSlots(c, "wout", Q_STORE, 2)
        r_st = StoreSlots(c, "rst", Q_STORE, 2)
        xmid_st = StoreSlots(c, "xmst", Q_STORE, 2)
        q_st = StoreSlots(c, "qst", Q_STORE, 2)
        h_st = StoreSlots(c, "hst", Q_STORE, 2)
        o_st = StoreSlots(c, "ost", Q_STORE, 1)
        sd_st = StoreSlots(c, "sdst", Q_STORE, 1)

        AR1b = AR1[:, :]
        AR2b = AR2[:, :]
        AR3b = AR3[:, :]
        A_buf = AR1b[:, :16 * T].rearrange("p (a t) -> p a t", a=32)
        w2res = AR2b[:, :32 * D].rearrange("p (a t) -> p a t", a=32)
        n_fm = AR3b.rearrange("p (a t) -> p a t", a=DC)
        m_fm = n_fm
        xq_fm = n_fm
        W1res = AR1b[:, :DC * T].rearrange("p (a t) -> p a t", a=DC)
        W2res = AR1b[:, DC * T:2 * DC * T].rearrange("p (a t) -> p a t", a=DC)
        ntm_res = AR2b[:, :TC * D].rearrange("p (a t) -> p a t", a=TC)
        g_t = AR2b[:, TC * D:2 * TC * D].rearrange("p (a t) -> p a t", a=TC)
        W3Tb = CH[:, :].bitcast(BF16).rearrange("p (a t) -> p a t", a=16)
        CHs = [CH[:, j * D:(j + 1) * D] for j in range(4)]
        CHbf = [CH[:, j * D:(j + 1) * D].bitcast(BF16) for j in range(4)]
        AR1f = AR1b.bitcast(F32)
        dq_f32 = AR1f[:, :I]
        dq_q = AR1b[:, 2 * I:3 * I]
        qtile = AR1b[:, 4 * I:4 * I + 2 * DC * 128].rearrange("p (s a t) -> p s a t", s=2, a=DC)
        qwdT = AR2b[:, :IC * D].rearrange("p (a t) -> p a t", a=IC)
        hbuf_a = AR1f[:, :T]
        hbuf_b = AR1f[:, T:2 * T]
        hbuf_c = AR1b[:, 8 * T:10 * T].rearrange("p (s t) -> p s t", s=2)
        def wst_dst(slot):
            return WS[:, slot // 2, :, (slot % 2) * 128:(slot % 2) * 128 + 128]

        def wst_tile(slot, a):
            return WS[:, slot // 2, a, (slot % 2) * 128:(slot % 2) * 128 + 128]
        hq_a = AR1b[:, :16 * T].rearrange("p (a t) -> p a t", a=16)
        hq_b = AR3b[:, :6 * T].rearrange("p (a t) -> p a t", a=6)

        def hq_row(ic):
            return hq_a[:, ic, :] if ic < 16 else hq_b[:, ic - 16, :]

        def hq_tile(ic, tcc):
            r = hq_row(ic)
            return r[:, tcc * 128:(tcc + 1) * 128]

        psA = [PS_A[:, j * 512:(j + 1) * 512] for j in range(4)]
        psB = [PS_B[:, j * 512:(j + 1) * 512] for j in range(3)]
        psT2 = [PS_A[:, 0:1024], PS_A[:, 1024:2048]]
        psTR = [PS_TR[:, j, :] for j in range(8)]
        psTRf = PS_TR[:, :, :].rearrange("p a t -> p (a t)").bitcast(F32)

        try:
            c.dma(Q_LOAD, g_bc[:, :], lng_h.partition_broadcast(128), dS)
            c.dma(Q_LOAD, b_bc[:, :], lnb_h.partition_broadcast(128), dS)
            c.dma(Q_LOAD, clipsb[0:1, 0:1], clipx_h.unsqueeze(0), dS)
            c.dma(Q_LOAD, clipsb[0:1, 1:2], clipd_h.unsqueeze(0), dS)
            c.dma(Q_CAST, fmb2row[0:1, :], fmb2_h.unsqueeze(0), dG)
            c.op("vector", lambda e: e.memset(ones_b[:, :], 1.0), inc=False)
            tm = c.op("vector", lambda e: e.memset(onesrow_b[0:1, :], 1.0))
            c.wp("gpsimd", "vector", tm)
            c.op("gpsimd", lambda e: e.affine_select(
                out=ident_b[:, :], in_=ones_b[:, :], pattern=[[1, 128]],
                compare_op=ALU.is_equal, fill=0.0, base=0, channel_multiplier=-1), inc=False)
            tg = c.op("gpsimd", lambda e: e.affine_select(
                out=trimask[:, :], in_=ones_b[:, :], pattern=[[1, 128]],
                compare_op=ALU.is_ge, fill=0.0, base=0, channel_multiplier=-1))
            c.wp("vector", "gpsimd", tg)
            c.op("vector", lambda e: e.tensor_copy(ident_f[:, :], ident_b[:, :]), inc=False)
            c.barrier()

            pcvec_state = {"last_tr": None, "last_ev": None}

            def load_pcvec(b_ap, n, dst):
                if pcvec_state["last_tr"] is not None:
                    c.wp(Q_LOAD, "tensor", pcvec_state["last_tr"])
                t1 = c.dma(Q_LOAD, brow[0:n, :], b_ap.rearrange("(c p) -> c p", p=128), dS)
                jt = c.qjoin(Q_LOAD)
                c.wp("tensor", Q_LOAD, jt)
                if pcvec_state["last_ev"] is not None:
                    c.wp("tensor", "vector", pcvec_state["last_ev"])
                tt = c.op("tensor", lambda e, n=n: e.transpose(PS_B[:, 0:n], brow[0:n, 0:128], ident_f[0:n, 0:n]))
                pcvec_state["last_tr"] = tt
                c.wp("vector", "tensor", tt)
                ev = c.op("vector", lambda e, n=n, d=dst: e.tensor_copy(d[:, 0:n], PS_B[:, 0:n]))
                pcvec_state["last_ev"] = ev
                return ev

            def ln_stats(buf, rs_col, nmrs_col, mu_col=None):
                c.op("vector", lambda e, b=buf: e.bn_stats(st6[:, 0:6], b[:, 0:512]), inc=False)
                c.op("vector", lambda e, b=buf: e.bn_stats(st6[:, 6:12], b[:, 512:1024]), inc=False)
                c.op("vector", lambda e: e.bn_aggr(mv2[:, 0:2], st6[:, 0:12]), inc=False)
                if mu_col is not None:
                    c.op("vector", lambda e, m=mu_col: e.tensor_copy(m, mv2[:, 0:1]), inc=False)
                tv = c.op("vector", lambda e: e.tensor_scalar(tmp1[:, 0:1], mv2[:, 1:2], EPS, None, ALU.add))
                c.wp("scalar", "vector", tv)
                ts_ = c.op("scalar", lambda e: e.activation(tmp2[:, 0:1], tmp1[:, 0:1], AF.Sqrt))
                c.wp("vector", "scalar", ts_)
                c.op("vector", lambda e, r=rs_col: e.reciprocal(r, tmp2[:, 0:1]), inc=False)
                c.op("vector", lambda e, r=rs_col: e.tensor_tensor(tmp3[:, 0:1], mv2[:, 0:1], r, ALU.mult), inc=False)
                return c.op("vector", lambda e, nm=nmrs_col: e.tensor_scalar(nm, tmp3[:, 0:1], -1.0, None, ALU.mult))

            def ln_stats2(buf):
                c.op("vector", lambda e, b=buf: e.bn_stats(st6[:, 0:6], b[:, 0:512]), inc=False)
                c.op("vector", lambda e, b=buf: e.bn_stats(st6[:, 6:12], b[:, 512:1024]), inc=False)
                c.op("vector", lambda e: e.bn_aggr(mv2[:, 0:2], st6[:, 0:12]), inc=False)
                tv = c.op("vector", lambda e: e.tensor_scalar(tmp1[:, 0:1], mv2[:, 1:2], EPS, None, ALU.add))
                c.wp("scalar", "vector", tv)
                ts_ = c.op("scalar", lambda e: e.activation(tmp2[:, 0:1], tmp1[:, 0:1], AF.Sqrt))
                c.wp("vector", "scalar", ts_)
                c.op("vector", lambda e: e.reciprocal(tmpr[:, 0:1], tmp2[:, 0:1]), inc=False)
                c.op("vector", lambda e: e.tensor_tensor(tmp3[:, 0:1], mv2[:, 0:1], tmpr[:, 0:1], ALU.mult), inc=False)
                return c.op("vector", lambda e: e.tensor_scalar(tmp3[:, 0:1], tmp3[:, 0:1], -1.0, None, ALU.mult))

            p0_free = {}
            last_tr = None
            for tc in range(TC):
                xbuf = CHs[tc % 2]
                ssem, stgt = xs.fill(xbuf, x_h[tc * 128:(tc + 1) * 128, :], free=p0_free.get(tc - 2))
                c.wait("vector", ssem, stgt)
                tvs = ln_stats(xbuf, rsx_all[:, tc:tc + 1], nmrs_all[:, tc:tc + 1], mu_all[:, tc:tc + 1])
                c.op("vector", lambda e, t=tc: e.tensor_tensor(tmp1[:, 0:1], mv2[:, 1:2], rsx_all[:, t:t + 1], ALU.mult), inc=False)
                c.op("vector", lambda e, t=tc: e.tensor_tensor(tmp1[:, 0:1], tmp1[:, 0:1], rsx_all[:, t:t + 1], ALU.mult), inc=False)
                tv = c.op("vector", lambda e: e.tensor_scalar(tmp1[:, 0:1], tmp1[:, 0:1], EPS, None, ALU.add))
                c.wp("scalar", "vector", tv)
                ts_ = c.op("scalar", lambda e: e.activation(tmp2[:, 0:1], tmp1[:, 0:1], AF.Sqrt))
                c.wp("vector", "scalar", ts_)
                c.op("vector", lambda e: e.reciprocal(tmp3[:, 0:1], tmp2[:, 0:1]), inc=False)
                tv = c.op("vector", lambda e, t=tc: e.tensor_tensor(rsc_all[:, t:t + 1], rsx_all[:, t:t + 1], tmp3[:, 0:1], ALU.mult))
                c.op("vector", lambda e, t=tc: e.tensor_tensor(tmp3[:, 0:1], mu_all[:, t:t + 1], rsc_all[:, t:t + 1], ALU.mult), inc=False)
                tv = c.op("vector", lambda e, t=tc: e.tensor_scalar(nmrsc_all[:, t:t + 1], tmp3[:, 0:1], -1.0, None, ALU.mult))
                c.wp("scalar", "vector", tv)
                ts_ = c.op("scalar", lambda e, xb=xbuf, t=tc: e.activation(
                    CHs[2][:, :], xb, AF.Identity,
                    bias=nmrsc_all[:, t:t + 1], scale=rsc_all[:, t:t + 1]))
                p0_free[tc] = ("p_scalar", ts_)
                c.wp("vector", "scalar", ts_)
                if last_tr is not None:
                    c.wp("vector", "tensor", last_tr)
                    c.wait("vector", ntm_st.sl[0], c.cnt[ntm_st.sl[0]])
                c.op("vector", lambda e: e.tensor_tensor(CHs[2][:, :], CHs[2][:, :], g_bc[:, :], ALU.mult), inc=False)
                tv_n = c.op("vector", lambda e: e.tensor_tensor(nchunk[:, :], CHs[2][:, :], b_bc[:, :], ALU.add))
                ntm_st.store(ntm_d[tc * 128:(tc + 1) * 128, :], nchunk[:, :], "vector", tv_n)
                c.wp("tensor", "vector", tv_n)
                for dc in range(DC):
                    tt = c.op("tensor", lambda e, dcc=dc: e.transpose(
                        psTR[dcc], nchunk[:, dcc * 128:(dcc + 1) * 128], ident_b[:, :]), inc=(dc == DC - 1))
                    if dc == DC - 1:
                        last_tr = tt
                c.wp("vector", "tensor", last_tr)
                for dc in range(DC):
                    c.op("vector", lambda e, dcc=dc, t=tc: e.tensor_copy(
                        n_fm[:, dcc, t * 128:(t + 1) * 128], psTR[dcc]), inc=False)
            c.barrier()
            if "ntm" in tap_outs:
                c.dma(Q_LOAD, tap_outs["ntm"], ntm_d, dS); c.qjoin(Q_LOAD); c.barrier()
            if "nfm" in tap_outs:
                for dcc in range(DC):
                    c.dma(Q_LOAD, tap_outs["nfm"][dcc * 128:(dcc + 1) * 128, :], n_fm[:, dcc, :], dS)
                c.qjoin(Q_LOAD); c.barrier()

            ckstop(1)

            def mlp_phase(w1_ap, b1_ap, w2_ap, b2_ap, src_fm, wout_d, fm_mode=False):
                tvb = load_pcvec(b1_ap, 32, bvec32)
                if not fm_mode:
                    tvb = load_pcvec(b2_ap, DC, bvec8)
                c.barrier()
                for half in range(2):
                    toff = half * TH
                    ev_done = {}
                    slot_last_use = [None, None]
                    for sc in range(DI // 256):
                        src = w1_ap.rearrange("(dc p) m -> p dc m", p=128)[:, :, sc * 256:(sc + 1) * 256]
                        wsl = sc % 2
                        ssem, stgt = ws.fill(WS[:, wsl, :, :], src, free=slot_last_use[wsl])
                        for k in range(2):
                            dic = sc * 2 + k
                            pair = dic % 2
                            c.wait("tensor", ssem, stgt)
                            if dic >= 2:
                                c.wp("tensor", "scalar", ev_done[dic - 2])
                            last = None
                            for tsp in range(NTS):
                                bank = psA[pair * 2 + tsp]
                                for dc in range(DC):
                                    isl = (tsp == NTS - 1 and dc == DC - 1)
                                    def mmf(e, dcc=dc, tspp=tsp, kk=k, bank=bank, wsl=wsl, toff=toff):
                                        return e.matmul(
                                            bank, WS[:, wsl, dcc, kk * 128:(kk + 1) * 128],
                                            src_fm[:, dcc, toff + tspp * 512: toff + tspp * 512 + 512],
                                            start=(dcc == 0), stop=(dcc == DC - 1))
                                    last = c.op("tensor", mmf, inc=isl)
                            if k == 1:
                                slot_last_use[wsl] = ("p_tensor", last)
                            c.wp("scalar", "tensor", last)
                            for tsp in range(NTS):
                                bank = psA[pair * 2 + tsp]
                                ev = c.op("scalar", lambda e, dicc=dic, tspp=tsp, bank=bank:
                                          e.activation(A_buf[:, dicc, tspp * 512: tspp * 512 + 512],
                                                       bank, AF.Gelu, bias=bvec32[:, dicc:dicc + 1]),
                                          inc=(tsp == NTS - 1))
                            ev_done[dic] = ev
                    wr_tgts = {}
                    for rsc in range(DI // 512):
                        src = w2_ap.rearrange("(a p) d -> p a d", p=128)[:, rsc * 4:(rsc + 1) * 4, :]
                        ssem, stgt = wrs.fill(w2res[:, rsc * 4:(rsc + 1) * 4, :], src)
                        wr_tgts[ssem] = stgt
                    c.wp("tensor", "scalar", ev_done[DIC - 1])
                    for ssem, stgt in wr_tgts.items():
                        c.wait("tensor", ssem, stgt)
                    if not fm_mode:
                        ev_w = {}
                        st_prev = {}
                        last_w = None
                        for dc in range(DC):
                            for tsp in range(NTS):
                                key = dc * NTS + tsp
                                bank = psB[key % 2]
                                if key >= 2:
                                    c.wp("tensor", "scalar", ev_w[key - 2])
                                for dic in range(DIC):
                                    def mmf(e, dicc=dic, dcc=dc, tspp=tsp, bank=bank):
                                        return e.matmul(
                                            bank, w2res[:, dicc, dcc * 128:(dcc + 1) * 128],
                                            A_buf[:, dicc, tspp * 512: tspp * 512 + 512],
                                            start=(dicc == 0), stop=(dicc == DIC - 1))
                                    last_w = c.op("tensor", mmf, inc=(dic == DIC - 1))
                                c.wp("scalar", "tensor", last_w)
                                esl = key % 2
                                if key >= 2:
                                    sem, tgt = st_prev[key - 2]
                                    c.wait("scalar", sem, tgt)
                                ev = c.op("scalar", lambda e, bank=bank, esl=esl, dcc=dc:
                                          e.activation(evb[:, esl, :], bank, AF.Identity,
                                                       bias=bvec8[:, dcc:dcc + 1]))
                                ev_w[key] = ev
                                st_prev[key] = wout_st.store(
                                    wout_d[dc][:, toff + tsp * 512: toff + tsp * 512 + 512],
                                    evb[:, esl, :], "scalar", ev)
                        lw_final = last_w
                    else:
                        fev = {}
                        rfree = {}
                        for tk in range(TH // 128):
                            tcg = half * (TH // 128) + tk
                            bank2 = psT2[tk % 2]
                            if tk >= 2:
                                c.wp("tensor", "vector", fev[tk - 2])
                            last = None
                            for dic in range(DIC):
                                for nsp in range(2):
                                    def mmf(e, dicc=dic, nspp=nsp, bank2=bank2, tkk=tk):
                                        return e.matmul(
                                            bank2[:, nspp * 512:(nspp + 1) * 512],
                                            A_buf[:, dicc, tkk * 128:(tkk + 1) * 128],
                                            w2res[:, dicc, nspp * 512:(nspp + 1) * 512],
                                            start=(dicc == 0), stop=False)
                                    c.op("tensor", mmf, inc=False)
                            for nsp in range(2):
                                def mmb(e, nspp=nsp, bank2=bank2):
                                    return e.matmul(
                                        bank2[:, nspp * 512:(nspp + 1) * 512],
                                        onesrow_b[0:1, :], fmb2row[0:1, nspp * 512:(nspp + 1) * 512],
                                        start=False, stop=True)
                                last = c.op("tensor", mmb, inc=(nsp == 1))
                            rsl = CHs[tk % 2]
                            ssem, stgt = xs.fill(rsl, r_d[tcg * 128:(tcg + 1) * 128, :], free=rfree.get(tk - 2))
                            c.wp("vector", "tensor", last)
                            c.wait("vector", ssem, stgt)
                            if tk >= 1:
                                sem, tgt = xm_prev[tk - 1]
                                c.wait("vector", sem, tgt)
                            tv = c.op("vector", lambda e, bank2=bank2, rb=rsl: e.tensor_tensor(
                                CHs[2][:, :], bank2[:, 0:1024], rb, ALU.add))
                            fev[tk] = tv
                            rfree[tk] = ("p_vector", tv)
                            xm_prev[tk] = xmid_st.store(
                                xmid_d[tcg * 128:(tcg + 1) * 128, :], CHs[2][:, :], "vector", tv)
                            tvs = ln_stats(CHs[2][:, :], rs1_all[:, tcg:tcg + 1],
                                           nmrs1_all[:, tcg:tcg + 1], mu1_all[:, tcg:tcg + 1])
                            c.wp("scalar", "vector", tvs)
                            ts_ = c.op("scalar", lambda e, t=tcg: e.activation(
                                CHs[3][:, :], CHs[2][:, :], AF.Identity,
                                bias=nmrs1_all[:, t:t + 1], scale=rs1_all[:, t:t + 1]))
                            c.wp("vector", "scalar", ts_)
                            c.op("vector", lambda e, t=tcg: e.tensor_reduce(
                                colmax[:, t:t + 1], CHs[3][:, :], axis=AX.X, op=ALU.max,
                                apply_absolute_value=True), inc=False)
                        lw_final = last
                    c.barrier()

            xm_prev = {}
            mlp_phase(h1w1_h, h1b1_h, h1w2_h, h1b2_h, n_fm, w1_d)
            if "w1" in tap_outs:
                c.dma(Q_LOAD, tap_outs["w1"], w1_d, dS); c.qjoin(Q_LOAD); c.barrier()
            ckstop(2)
            mlp_phase(h2w1_h, h2b1_h, h2w2_h, h2b2_h, n_fm, w2_d)
            if "w2" in tap_outs:
                c.dma(Q_LOAD, tap_outs["w2"], w2_d, dS); c.qjoin(Q_LOAD); c.barrier()

            ckstop(3)
            for dcc in range(DC):
                c.dma(Q_LOAD, W1res[:, dcc, :], w1_d[dcc], dS)
                c.dma(Q_LOAD, W2res[:, dcc, :], w2_d[dcc], dS)
            for tcc in range(TC):
                c.dma(Q_LOAD, ntm_res[:, tcc, :],
                      ntm_d.rearrange("(a p) d -> p a d", p=128)[:, tcc, :], dS)
            jt = c.qjoin(Q_LOAD)
            for e_ in ("tensor", "vector", "scalar"):
                c.wp(e_, Q_LOAD, jt)
            for zs in range(ZS):
                smax = 4 * zs + 3
                sev = {}
                for sc in range(smax + 1):
                    bank = psB[sc % 2]
                    if sc >= 2:
                        c.wp("tensor", "vector", sev[sc - 2])
                    last = None
                    for dc in range(DC):
                        def mmf(e, dcc=dc, scc=sc, zss=zs, bank=bank):
                            return e.matmul(bank, W2res[:, dcc, scc * 128:(scc + 1) * 128],
                                            W1res[:, dcc, zss * 512:zss * 512 + 512],
                                            start=(dcc == 0), stop=(dcc == DC - 1))
                        last = c.op("tensor", mmf, inc=(dc == DC - 1))
                    c.wp("vector", "tensor", last)
                    if sc < 4 * zs:
                        tv = c.op("vector", lambda e, scc=sc, bank=bank: e.tensor_copy(W3Tb[:, scc, :], bank))
                    else:
                        tv = None
                        for j in range(4):
                            zc = zs * 4 + j
                            if zc < sc:
                                continue
                            if zc == sc:
                                tv = c.op("vector", lambda e, scc=sc, jj=j, bank=bank: e.tensor_tensor(
                                    W3Tb[:, scc, jj * 128:(jj + 1) * 128],
                                    bank[:, jj * 128:(jj + 1) * 128], trimask[:, :], ALU.mult))
                            else:
                                tv = c.op("vector", lambda e, scc=sc, jj=j, bank=bank: e.tensor_copy(
                                    W3Tb[:, scc, jj * 128:(jj + 1) * 128],
                                    bank[:, jj * 128:(jj + 1) * 128]))
                    sev[sc] = tv
                gev = {}
                for j in range(4):
                    zc = zs * 4 + j
                    bank2 = psT2[j % 2]
                    if j >= 2:
                        c.wp("tensor", "scalar", gev[j - 2])
                    c.wp("tensor", "vector", sev[min(zc, smax)])
                    last = None
                    for sc in range(zc + 1):
                        for nsp in range(2):
                            def mmf(e, scc=sc, jj=j, nspp=nsp, bank2=bank2, zcc=zc):
                                return e.matmul(
                                    bank2[:, nspp * 512:(nspp + 1) * 512],
                                    W3Tb[:, scc, jj * 128:(jj + 1) * 128],
                                    ntm_res[:, scc, nspp * 512:(nspp + 1) * 512],
                                    start=(scc == 0), stop=(scc == zcc))
                            last = c.op("tensor", mmf, inc=(sc == zc and nsp == 1))
                    c.wp("scalar", "tensor", last)
                    gev[j] = c.op("scalar", lambda e, zcc=zc, bank2=bank2: e.activation(
                        g_t[:, zcc, :], bank2[:, 0:1024], AF.Gelu))
                c.barrier()
            if "gt" in tap_outs:
                for tcc in range(TC):
                    c.dma(Q_LOAD, tap_outs["gt"][tcc * 128:(tcc + 1) * 128, :], g_t[:, tcc, :], dS)
                c.qjoin(Q_LOAD); c.barrier()

            ckstop(4)
            r_prev = {}
            last_tr = None
            tvm_prev = None
            for tc in range(TC):
                xbuf = CHs[tc % 2]
                ssem, stgt = xs.fill(xbuf, x_h[tc * 128:(tc + 1) * 128, :], free=r_prev.get(tc - 2))
                c.wait("scalar", ssem, stgt)
                if tvm_prev is not None:
                    c.wp("scalar", "vector", tvm_prev)
                ts_y = c.op("scalar", lambda e, xb=xbuf, t=tc: e.activation(
                    CHs[2][:, :], xb, AF.Identity,
                    bias=nmrs_all[:, t:t + 1], scale=rsx_all[:, t:t + 1]))
                c.wp("vector", "scalar", ts_y)
                tv_r = c.op("vector", lambda e, xb=xbuf: e.tensor_tensor(xb, xb, CHs[2][:, :], ALU.add))
                r_prev[tc] = r_st.store(r_d[tc * 128:(tc + 1) * 128, :], xbuf, "vector", tv_r)
                tvs = ln_stats2(g_t[:, tc, :])
                c.wp("scalar", "vector", tvs)
                ts_ = c.op("scalar", lambda e, t=tc: e.activation(
                    CHs[3][:, :], g_t[:, t, :], AF.Identity, bias=tmp3[:, 0:1], scale=tmpr[:, 0:1]))
                c.wp("vector", "scalar", ts_)
                c.op("vector", lambda e: e.tensor_tensor(CHs[3][:, :], CHs[3][:, :], g_bc[:, :], ALU.mult), inc=False)
                c.op("vector", lambda e: e.tensor_tensor(CHs[3][:, :], CHs[3][:, :], b_bc[:, :], ALU.add), inc=False)
                c.op("vector", lambda e: e.tensor_tensor(CHs[3][:, :], CHs[3][:, :], CHs[2][:, :], ALU.add), inc=False)
                tvs = ln_stats2(CHs[3][:, :])
                c.wp("scalar", "vector", tvs)
                ts_ = c.op("scalar", lambda e: e.activation(
                    CHs[2][:, :], CHs[3][:, :], AF.Identity, bias=tmp3[:, 0:1], scale=tmpr[:, 0:1]))
                c.wp("vector", "scalar", ts_)
                if last_tr is not None:
                    c.wp("vector", "tensor", last_tr)
                c.op("vector", lambda e: e.tensor_tensor(CHs[2][:, :], CHs[2][:, :], g_bc[:, :], ALU.mult), inc=False)
                tv_m = c.op("vector", lambda e: e.tensor_tensor(nchunk[:, :], CHs[2][:, :], b_bc[:, :], ALU.add))
                tvm_prev = tv_m
                c.wp("tensor", "vector", tv_m)
                for dc in range(DC):
                    last_tr = c.op("tensor", lambda e, dcc=dc: e.transpose(
                        psTR[dcc], nchunk[:, dcc * 128:(dcc + 1) * 128], ident_b[:, :]), inc=(dc == DC - 1))
                c.wp("vector", "tensor", last_tr)
                for dc in range(DC):
                    c.op("vector", lambda e, dcc=dc, t=tc: e.tensor_copy(
                        m_fm[:, dcc, t * 128:(t + 1) * 128], psTR[dcc]), inc=False)
            c.barrier()
            if "r" in tap_outs:
                c.dma(Q_LOAD, tap_outs["r"], r_d, dS); c.qjoin(Q_LOAD); c.barrier()
            if "mfm" in tap_outs:
                for dcc in range(DC):
                    c.dma(Q_LOAD, tap_outs["mfm"][dcc * 128:(dcc + 1) * 128, :], m_fm[:, dcc, :], dS)
                c.qjoin(Q_LOAD); c.barrier()

            ckstop(5)
            mlp_phase(fmw1_h, fmb1_h, fmw2_h, fmb2_h, m_fm, None, fm_mode=True)
            if "xmid" in tap_outs:
                c.dma(Q_LOAD, tap_outs["xmid"], xmid_d, dS); c.qjoin(Q_LOAD); c.barrier()

            ckstop(6)
            def global_scalar(src_cols, ncols, cci, cco, clip_col, sv, rv, scr_off):
                tv = c.op("vector", lambda e: e.memset(cl8[:, :], 0.0))
                tv = c.op("vector", lambda e, n=ncols, s=src_cols: e.tensor_reduce(
                    cl8[:, 0:1], s, axis=AX.X, op=ALU.max))
                c.wp("gpsimd", "vector", tv)
                c.dma(Q_CAST, cci.ap(), cl8[:, :], dG)
                jt = c.qjoin(Q_CAST)
                ct = c.op("gpsimd", lambda e, ci=cci, co=cco: e.collective_compute(
                    "AllReduce", ALU.max, replica_groups=RG8,
                    ins=[ci.ap().opt()], outs=[co.ap().opt()]))
                c.wait("gpsimd", "p_gpsimd", ct)
                c.dma(Q_CAST, ggrow[0:1, :], cco.ap()[:, 0:1].rearrange("p a -> a p"), dG, slow=True)
                jt = c.qjoin(Q_CAST)
                c.wp("vector", Q_CAST, jt)
                tv = c.op("vector", lambda e: e.tensor_reduce(gsc[0:1, 0:1], ggrow[0:1, :], axis=AX.X, op=ALU.max))
                c.wp("scalar", "vector", tv)
                ts_ = c.op("scalar", lambda e, cc=clip_col: e.activation(gsc[0:1, 1:2], clipsb[0:1, cc:cc + 1], AF.Sigmoid))
                c.wp("vector", "scalar", ts_)
                c.op("vector", lambda e: e.tensor_tensor(gsc[0:1, 2:3], gsc[0:1, 0:1], gsc[0:1, 1:2], ALU.mult), inc=False)
                c.op("vector", lambda e: e.tensor_scalar(gsc[0:1, 2:3], gsc[0:1, 2:3], 1.0 / QMAX, None, ALU.mult), inc=False)
                tv = c.op("vector", lambda e: e.reciprocal(gsc[0:1, 3:4], gsc[0:1, 2:3]))
                c.wp("gpsimd", "vector", tv)
                c.dma(Q_CAST, scr8_d[scr_off:scr_off + 2], gsc[0:1, 2:4], dG)
                jt = c.qjoin(Q_CAST)
                c.dma(Q_CAST, sv[:, 0:1], scr8_d[scr_off:scr_off + 1].partition_broadcast(128), dG)
                c.dma(Q_CAST, rv[:, 0:1], scr8_d[scr_off + 1:scr_off + 2].partition_broadcast(128), dG)
                c.barrier()

            global_scalar(colmax[:, 0:TC], TC, cc1i, cc1o, 0, sx_vec, rsx_q, 0)

            ckstop(7)
            qfree = {}
            qprev = {}
            nq = 0
            for (w_ap, dst, sall) in ((wg_h, qwg_d, sg_all), (wu_h, qwu_d, su_all)):
                for ic in range(IC):
                    wbuf = CHs[nq % 2]
                    ssem, stgt = xs.fill(wbuf, w_ap[ic * 128:(ic + 1) * 128, :], free=qfree.get(nq - 2))
                    c.wait("vector", ssem, stgt)
                    c.op("vector", lambda e, wb=wbuf: e.tensor_reduce(
                        tmp1[:, 0:1], wb, axis=AX.X, op=ALU.max, apply_absolute_value=True), inc=False)
                    c.op("vector", lambda e, sa=sall, icc=ic: e.tensor_scalar(
                        sa[:, icc:icc + 1], tmp1[:, 0:1], 1.0 / QMAX, None, ALU.mult), inc=False)
                    c.op("vector", lambda e, sa=sall, icc=ic: e.reciprocal(tmp2[:, 0:1], sa[:, icc:icc + 1]), inc=False)
                    c.op("vector", lambda e, wb=wbuf: e.tensor_scalar(
                        CHs[2][:, :], wb, tmp2[:, 0:1], RND, ALU.mult, ALU.add), inc=False)
                    tv_q = c.op("vector", lambda e: e.tensor_scalar(
                        nchunk[:, :], CHs[2][:, :], -RND, None, ALU.add))
                    qfree[nq] = ("p_vector", tv_q)
                    c.wp("tensor", "vector", tv_q)
                    ltr = None
                    for dc in range(DC):
                        ltr = c.op("tensor", lambda e, dcc=dc: e.transpose(
                            psTR[dcc], nchunk[:, dcc * 128:(dcc + 1) * 128], ident_b[:, :]), inc=(dc == DC - 1))
                    c.wp("vector", "tensor", ltr)
                    if nq >= 2:
                        sem, tgt = qprev[nq - 2]
                        c.wait("vector", sem, tgt)
                    tv = None
                    for dc in range(DC):
                        tv = c.op("vector", lambda e, dcc=dc, qs=nq % 2: e.tensor_copy(
                            qtile[:, qs, dcc, :], psTR[dcc]), inc=(dc == DC - 1))
                    qprev[nq] = q_st.store(dst[ic], qtile[:, nq % 2, :, :], "vector", tv)
                    nq += 1
            c.barrier()

            ckstop(8)
            evtr = {}
            for hc in range(DC):
                t1 = c.dma(Q_LOAD, dq_f32[:, :], wd_h[hc * 128:(hc + 1) * 128, :], dDQ)
                c.wait("vector", dDQ, t1)
                c.op("vector", lambda e: e.tensor_reduce(
                    tmp1[:, 0:1], dq_f32[:, :], axis=AX.X, op=ALU.max, apply_absolute_value=True), inc=False)
                c.op("vector", lambda e, h=hc: e.tensor_scalar(
                    sd_all[:, h:h + 1], tmp1[:, 0:1], 1.0 / QMAX, None, ALU.mult), inc=False)
                c.op("vector", lambda e, h=hc: e.reciprocal(tmp2[:, 0:1], sd_all[:, h:h + 1]), inc=False)
                c.op("vector", lambda e: e.tensor_scalar(
                    dq_f32[:, :], dq_f32[:, :], tmp2[:, 0:1], RND, ALU.mult, ALU.add), inc=False)
                tv_q = c.op("vector", lambda e: e.tensor_scalar(
                    dq_q[:, :], dq_f32[:, :], -RND, None, ALU.add))
                c.wp("tensor", "vector", tv_q)
                for g0 in range(0, IC, 8):
                    g1 = min(g0 + 8, IC)
                    if g0 > 0:
                        c.wp("tensor", "vector", evtr[g0 - 1])
                    ltt = None
                    for i2 in range(g0, g1):
                        ltt = c.op("tensor", lambda e, i2=i2: e.transpose(
                            psTR[i2 % 8], dq_q[:, i2 * 128:(i2 + 1) * 128], ident_b[:, :]),
                            inc=(i2 == g1 - 1))
                    c.wp("vector", "tensor", ltt)
                    for i2 in range(g0, g1):
                        evtr[i2] = c.op("vector", lambda e, i2=i2, h=hc: e.tensor_copy(
                            qwdT[:, i2, h * 128:(h + 1) * 128], psTR[i2 % 8]))
                c.barrier()
            ckstop(8.2)
            if "qwdT" in tap_outs:
                for i2 in range(IC):
                    c.dma(Q_LOAD, tap_outs["qwdT"][i2 * 128:(i2 + 1) * 128, :], qwdT[:, i2, :], dS)
                c.qjoin(Q_LOAD); c.barrier()

            tt = c.op("tensor", lambda e: e.transpose(PS_B[0:DC, 0:128], sd_all[:, 0:DC], ident_f[:, :]))
            c.wp("vector", "tensor", tt)
            tv = c.op("vector", lambda e: e.tensor_copy(brow[0:DC, 0:128], PS_B[0:DC, 0:128]))
            ckstop(8.5)
            sd_st.store(sdrow_d, brow[0:DC, 0:128], "vector", tv)
            jt = c.qjoin(Q_STORE)
            ckstop(8.7)
            c.wp(Q_LOAD, Q_STORE, jt)
            c.dma(Q_LOAD, sdh_bc[:, :], sdrow_d.rearrange("a q -> (a q)").partition_broadcast(128), dS)
            c.qjoin(Q_LOAD)
            c.barrier()

            ckstop(9)
            xqfree = {}
            last_tr = None
            xq_ch2_prev = None
            for tc in range(TC):
                xbuf = CHs[tc % 2]
                ssem, stgt = xs.fill(xbuf, xmid_d[tc * 128:(tc + 1) * 128, :], free=xqfree.get(tc - 2))
                c.wait("scalar", ssem, stgt)
                if xq_ch2_prev is not None:
                    c.wp("scalar", "vector", xq_ch2_prev)
                ts_ = c.op("scalar", lambda e, xb=xbuf, t=tc: e.activation(
                    CHs[2][:, :], xb, AF.Identity,
                    bias=nmrs1_all[:, t:t + 1], scale=rs1_all[:, t:t + 1]))
                xqfree[tc] = ("p_scalar", ts_)
                c.wp("vector", "scalar", ts_)
                if last_tr is not None:
                    c.wp("vector", "tensor", last_tr)
                xq_ch2_prev = c.op("vector", lambda e: e.tensor_scalar(
                    CHs[3][:, :], CHs[2][:, :], rsx_q[:, 0:1], RND, ALU.mult, ALU.add))
                c.op("vector", lambda e: e.tensor_scalar(
                    CHs[3][:, :], CHs[3][:, :], -RND, QMAX, ALU.add, ALU.min), inc=False)
                tv_q = c.op("vector", lambda e: e.tensor_scalar(
                    nchunk[:, :], CHs[3][:, :], -(QMAX + 1.0), None, ALU.max))
                c.wp("tensor", "vector", tv_q)
                for dc in range(DC):
                    last_tr = c.op("tensor", lambda e, dcc=dc: e.transpose(
                        psTR[dcc], nchunk[:, dcc * 128:(dcc + 1) * 128], ident_b[:, :]), inc=(dc == DC - 1))
                c.wp("vector", "tensor", last_tr)
                for dc in range(DC):
                    c.op("vector", lambda e, dcc=dc, t=tc: e.tensor_copy(
                        xq_fm[:, dcc, t * 128:(t + 1) * 128], psTR[dcc]), inc=False)
            c.barrier()
            if "xq" in tap_outs:
                for dcc in range(DC):
                    c.dma(Q_LOAD, tap_outs["xq"][dcc * 128:(dcc + 1) * 128, :], xq_fm[:, dcc, :], dS)
                c.qjoin(Q_LOAD); c.barrier()

            ckstop(10)
            c.op("vector", lambda e: e.tensor_scalar(sg_all[:, :], sg_all[:, :], sx_vec[:, 0:1], None, ALU.mult), inc=False)
            tvc0 = c.op("vector", lambda e: e.tensor_scalar(su_all[:, :], su_all[:, :], sx_vec[:, 0:1], None, ALU.mult))
            c.wp("scalar", "vector", tvc0)
            sgev = {}
            hprev = {}
            hprev_tvb = {}
            gu_free = [None, None, None, None]
            for ic in range(IC):
                gslot = 2 * (ic % 2)
                uslot = gslot + 1
                gsem, gtgt = qws.fill(wst_dst(gslot), qwg_d[ic], free=gu_free[gslot])
                usem, utgt = qws.fill(wst_dst(uslot), qwu_d[ic], free=gu_free[uslot])
                c.wait("tensor", gsem, gtgt)
                if ic >= 1:
                    c.wp("tensor", "scalar", sgev[ic - 1][0])
                lastg = None
                for tsp in range(NTT):
                    for hc in range(DC):
                        def mmf(e, tspp=tsp, hcc=hc, gs=gslot):
                            return e.matmul(PS_A[:, tspp * 512:(tspp + 1) * 512],
                                            wst_tile(gs, hcc),
                                            xq_fm[:, hcc, tspp * 512:(tspp + 1) * 512],
                                            start=(hcc == 0), stop=(hcc == DC - 1))
                        lastg = c.op("tensor", mmf, inc=(tsp == NTT - 1 and hc == DC - 1))
                gu_free[gslot] = ("p_tensor", lastg)
                c.wp("scalar", "tensor", lastg)
                if ic >= 1:
                    c.wp("scalar", "vector", hprev_tvb[ic - 1])
                ev_g = c.op("scalar", lambda e, icc=ic: e.activation(
                    hbuf_a[:, :], PS_A[:, 0:T], AF.Silu, scale=sg_all[:, icc:icc + 1]))
                c.wait("tensor", usem, utgt)
                if ic >= 1:
                    c.wp("tensor", "scalar", sgev[ic - 1][1])
                lastu = None
                for tsp in range(NTT):
                    bank = psB[tsp] if tsp < 3 else psTRf
                    for hc in range(DC):
                        def mmf(e, tspp=tsp, hcc=hc, us=uslot, bank=bank):
                            return e.matmul(bank, wst_tile(us, hcc),
                                            xq_fm[:, hcc, tspp * 512:(tspp + 1) * 512],
                                            start=(hcc == 0), stop=(hcc == DC - 1))
                        lastu = c.op("tensor", mmf, inc=(tsp == NTT - 1 and hc == DC - 1))
                gu_free[uslot] = ("p_tensor", lastu)
                c.wp("scalar", "tensor", lastu)
                if NTT <= 3:
                    ev_u = c.op("scalar", lambda e, icc=ic: e.activation(
                        hbuf_b[:, :], PS_B[:, 0:T], AF.Identity, scale=su_all[:, icc:icc + 1]))
                else:
                    c.op("scalar", lambda e, icc=ic: e.activation(
                        hbuf_b[:, 0:1536], PS_B[:, 0:1536], AF.Identity, scale=su_all[:, icc:icc + 1]), inc=False)
                    ev_u = c.op("scalar", lambda e, icc=ic: e.activation(
                        hbuf_b[:, 1536:2048], psTRf, AF.Identity, scale=su_all[:, icc:icc + 1]))
                sgev[ic] = (ev_g, ev_u)
                c.wp("vector", "scalar", ev_u)
                c.op("vector", lambda e: e.tensor_tensor(hbuf_a[:, :], hbuf_a[:, :], hbuf_b[:, :], ALU.mult), inc=False)
                c.op("vector", lambda e, icc=ic: e.tensor_reduce(
                    hmax_all[:, icc:icc + 1], hbuf_a[:, :], axis=AX.X,
                    op=ALU.max, apply_absolute_value=True), inc=False)
                if ic >= 2:
                    sem, tgt = hprev[ic - 2]
                    c.wait("vector", sem, tgt)
                tvb = c.op("vector", lambda e, icc=ic: e.tensor_copy(hbuf_c[:, icc % 2, :], hbuf_a[:, :]))
                hprev_tvb[ic] = tvb
                hprev[ic] = h_st.store(hpre_d[ic], hbuf_c[:, ic % 2, :], "vector", tvb)
            c.barrier()
            if "hpre" in tap_outs:
                c.dma(Q_LOAD, tap_outs["hpre"], hpre_d, dS); c.qjoin(Q_LOAD); c.barrier()

            ckstop(11)
            global_scalar(hmax_all[:, 0:IC], IC, cc2i, cc2o, 1, sh_vec, rsh_q, 2)
            tv = c.op("vector", lambda e: e.tensor_scalar(
                sdh_bc[:, :], sdh_bc[:, :], sh_vec[:, 0:1], None, ALU.mult))
            c.barrier()

            ckstop(12)
            hqfree = {}
            tvq = None
            ch23 = CH[:, 2 * D:4 * D]
            for ic in range(IC):
                hbf = CHbf[ic % 2][:, 0:T]
                ssem, stgt = xs.fill(hbf, hpre_d[ic], free=hqfree.get(ic - 2))
                c.wait("vector", ssem, stgt)
                tv1 = c.op("vector", lambda e, hb=hbf: e.tensor_scalar(
                    ch23[:, 0:T], hb, rsh_q[:, 0:1], RND, ALU.mult, ALU.add))
                hqfree[ic] = ("p_vector", tv1)
                c.op("vector", lambda e: e.tensor_scalar(
                    ch23[:, 0:T], ch23[:, 0:T], -RND, QMAX, ALU.add, ALU.min), inc=False)
                tvq = c.op("vector", lambda e, icc=ic: e.tensor_scalar(
                    hq_row(icc), ch23[:, 0:T], -(QMAX + 1.0), None, ALU.max))
            c.wp("tensor", "vector", tvq)
            dev = {}
            oprev = {}
            xbfree = {-2: hqfree.get(IC - 2), -1: hqfree.get(IC - 1)}
            for tc in range(TC):
                bank2 = psT2[tc % 2]
                if tc >= 2:
                    c.wp("tensor", "vector", dev[tc - 2])
                last = None
                for ic in range(IC):
                    for nsp in range(2):
                        def mmf(e, icc=ic, nspp=nsp, tcc=tc, bank2=bank2):
                            return e.matmul(bank2[:, nspp * 512:(nspp + 1) * 512],
                                            hq_tile(icc, tcc),
                                            qwdT[:, icc, nspp * 512:(nspp + 1) * 512],
                                            start=(icc == 0), stop=(icc == IC - 1))
                        last = c.op("tensor", mmf, inc=(ic == IC - 1 and nsp == 1))
                xbuf = CHs[tc % 2]
                ssem, stgt = xs.fill(xbuf, xmid_d[tc * 128:(tc + 1) * 128, :], free=xbfree.get(tc - 2))
                c.wait("vector", ssem, stgt)
                c.wp("vector", "tensor", last)
                if tc >= 1:
                    sem, tgt = oprev[tc - 1]
                    c.wait("vector", sem, tgt)
                c.op("vector", lambda e, bank2=bank2: e.tensor_tensor(
                    CHs[2][:, :], bank2[:, 0:1024], sdh_bc[:, :], ALU.mult), inc=False)
                tv = c.op("vector", lambda e, xb=xbuf: e.tensor_tensor(
                    CHs[3][:, :], CHs[2][:, :], xb, ALU.add))
                dev[tc] = tv
                xbfree[tc] = ("p_vector", tv)
                oprev[tc] = o_st.store(out_h[tc * 128:(tc + 1) * 128, :], CHs[3][:, :], "vector", tv)
            c.wait(Q_STORE, o_st.sl[0], c.cnt[o_st.sl[0]])
            c.barrier()


        except _Stop:
            c.barrier()

        with nc.Block() as block:
            @block.tensor
            def _(e):
                for f in c.ops["tensor"]:
                    f(e)

            @block.vector
            def _(e):
                for f in c.ops["vector"]:
                    f(e)

            @block.scalar
            def _(e):
                for f in c.ops["scalar"]:
                    f(e)

            @block.gpsimd
            def _(e):
                for f in c.ops["gpsimd"]:
                    f(e)

            @block.sync
            def _(e):
                for f in c.ops["sync"]:
                    f(e)

    return nc




def kernel(**inputs):
    X = np.ascontiguousarray(np.asarray(inputs["L0_attn_x"], dtype=np.float32))
    B, T, D_ = X.shape
    nc = build_program(T=T)
    common = {
        "ln_g": inputs["L0_attn_ln_g"], "ln_b": inputs["L0_attn_ln_b"],
        "h1w1": inputs["L0_attn_h1_w1"], "h1b1": inputs["L0_attn_h1_b1"],
        "h1w2": inputs["L0_attn_h1_w2"], "h1b2": inputs["L0_attn_h1_b2"],
        "h2w1": inputs["L0_attn_h2_w1"], "h2b1": inputs["L0_attn_h2_b1"],
        "h2w2": inputs["L0_attn_h2_w2"], "h2b2": inputs["L0_attn_h2_b2"],
        "fmw1": inputs["L0_attn_fm_w1"], "fmb1": inputs["L0_attn_fm_b1"],
        "fmw2": inputs["L0_attn_fm_w2"], "fmb2": inputs["L0_attn_fm_b2"],
        "clip_x": inputs["L0_ffn_clip_x"], "clip_down": inputs["L0_ffn_clip_down"],
        "w_gate": inputs["L0_ffn_w_gate"], "w_up": inputs["L0_ffn_w_up"],
        "w_down": inputs["L0_ffn_w_down"],
    }
    common = {k: np.ascontiguousarray(np.asarray(v, dtype=np.float32)) for k, v in common.items()}
    in_maps = [dict(common, x=np.ascontiguousarray(X[b])) for b in range(B)]
    res = run_bass_kernel_spmd(nc, in_maps, core_ids=list(range(8)))
    return np.stack([res.results[b]["out"] for b in range(B)], axis=0)


# version 02 (final)
# speedup vs baseline: 1.1341x; 1.1341x over previous
import contextlib
import numpy as np
import ml_dtypes
import concourse.bass as bass
import concourse.mybir as mybir
from concourse.bass_utils import run_bass_kernel_spmd

F32 = mybir.dt.float32
BF16 = mybir.dt.bfloat16
AF = mybir.ActivationFunctionType
ALU = mybir.AluOpType
AX = mybir.AxisListType

RND = float(3 << 22)
EPS = 1e-5
QMAX = 127.0

D = 1024
DI = 4096
I = 2816
DC = D // 128
DIC = DI // 128
IC = I // 128

ENGINES = ("tensor", "vector", "scalar", "gpsimd", "sync")
Q_LOAD = "sync"
Q_STORE = "scalar"
Q_CAST = "gpsimd"
RG8 = [[0, 1, 2, 3, 4, 5, 6, 7]]


class Ctx:
    def __init__(self, nc, es):
        self.nc = nc
        self.es = es
        self.ops = {e: [] for e in ENGINES}
        self.sems = {}
        self.cnt = {}
        for e in ENGINES:
            self.new_sem("p_" + e)

    def new_sem(self, name):
        assert name not in self.sems
        self.sems[name] = self.es.enter_context(self.nc.semaphore(name))
        self.cnt[name] = 0
        return name

    def emit(self, eng, fn):
        self.ops[eng].append(fn)

    SERIAL = ("vector", "scalar", "gpsimd")

    def op(self, eng, fn, inc=True):
        name = "p_" + eng
        if eng in self.SERIAL:
            cur = self.cnt[name]
            if cur > 0:
                self.emit(eng, lambda e, s=self.sems[name], t=cur: e.wait_ge(s, t))
            self.cnt[name] += 1
            tgt = self.cnt[name]
            s = self.sems[name]
            self.emit(eng, lambda e, fn=fn, s=s: fn(e).then_inc(s, 1))
            return tgt
        if inc:
            self.cnt[name] += 1
            tgt = self.cnt[name]
            s = self.sems[name]
            self.emit(eng, lambda e, fn=fn, s=s: fn(e).then_inc(s, 1))
            return tgt
        self.emit(eng, lambda e, fn=fn: fn(e))
        return None

    def op_free(self, eng, fn, free_sem):
        self.cnt[free_sem] += 1
        s = self.sems[free_sem]
        self.emit(eng, lambda e, fn=fn, s=s: fn(e).then_inc(s, 1))
        return self.cnt[free_sem]

    def wait(self, eng, sem, tgt):
        if tgt is None or tgt <= 0:
            return
        self.emit(eng, lambda e, s=self.sems[sem], t=tgt: e.wait_ge(s, t))

    def wp(self, eng, other, tgt):
        self.wait(eng, "p_" + other, tgt)

    def dma(self, q, out_ap, in_ap, sem, slow=False):
        self.cnt[sem] += 16
        tgt = self.cnt[sem]
        s = self.sems[sem]
        self.emit(q, lambda e, o=out_ap, i=in_ap, s=s, sl=slow: e.dma_start(
            out=o, in_=i, allow_slow_non_contiguous=sl).then_inc(s, 16))
        return tgt

    def qjoin(self, q):
        for sname, cval in list(self.cnt.items()):
            if sname.startswith("d_" + q) and cval > 0:
                self.wait(q, sname, cval)
        name = "p_" + q
        self.cnt[name] += 1
        tgt = self.cnt[name]
        s = self.sems[name]
        self.emit(q, lambda e, s=s: e.sem_inc(s, 1))
        return tgt

    def barrier(self):
        tgts = {}
        for e in ENGINES:
            for sname, cval in list(self.cnt.items()):
                if sname.startswith("d_" + e) and cval > 0:
                    self.wait(e, sname, cval)
            name = "p_" + e
            self.cnt[name] += 1
            tgts[e] = self.cnt[name]
            s = self.sems[name]
            def f(eng, s=s):
                eng.drain()
                eng.sem_inc(s, 1)
            self.emit(e, f)
        for e in ENGINES:
            for o in ENGINES:
                if o != e:
                    self.wp(e, o, tgts[o])


class Stream:

    def __init__(self, ctx, name, q, nslots):
        self.c = ctx
        self.q = q
        self.n = nslots
        self.sl = [ctx.new_sem(f"d_{q}_{name}{j}") for j in range(nslots)]
        self.i = 0

    def fill(self, out_ap, in_ap, free=None):
        c = self.c
        if free is not None:
            sem, tgt = free
            c.wait(self.q, sem, tgt)
        s = self.sl[self.i % self.n]
        tgt = c.dma(self.q, out_ap, in_ap, s)
        self.i += 1
        return s, tgt


class StoreSlots:

    def __init__(self, ctx, name, q, nslots):
        self.c = ctx
        self.q = q
        self.n = nslots
        self.sl = [ctx.new_sem(f"d_{q}_{name}{j}") for j in range(nslots)]
        self.i = 0

    def store(self, out_ap, in_ap, ready_eng, ready_tgt):
        c = self.c
        c.wp(self.q, ready_eng, ready_tgt)
        s = self.sl[self.i % self.n]
        tgt = c.dma(self.q, out_ap, in_ap, s)
        self.i += 1
        return s, tgt


def build_program(T=2048, taps=(), stop=99):
    TC = T // 128
    TH = T // 2
    NTS = TH // 512
    NTT = T // 512
    ZS = T // 512

    nc = bass.Bass("TRN2", target_bir_lowering=False, debug=False)

    def din(name, shape):
        return nc.dram_tensor(name, shape, F32, kind="ExternalInput").ap()

    x_h = din("x", [T, D])
    lng_h = din("ln_g", [D]); lnb_h = din("ln_b", [D])
    h1w1_h = din("h1w1", [D, DI]); h1b1_h = din("h1b1", [DI])
    h1w2_h = din("h1w2", [DI, D]); h1b2_h = din("h1b2", [D])
    h2w1_h = din("h2w1", [D, DI]); h2b1_h = din("h2b1", [DI])
    h2w2_h = din("h2w2", [DI, D]); h2b2_h = din("h2b2", [D])
    fmw1_h = din("fmw1", [D, DI]); fmb1_h = din("fmb1", [DI])
    fmw2_h = din("fmw2", [DI, D]); fmb2_h = din("fmb2", [D])
    clipx_h = din("clip_x", [1]); clipd_h = din("clip_down", [1])
    wg_h = din("w_gate", [I, D]); wu_h = din("w_up", [I, D]); wd_h = din("w_down", [D, I])
    out_h = nc.dram_tensor("out", [T, D], F32, kind="ExternalOutput").ap()

    def dscr(name, shape, dt=BF16):
        return nc.dram_tensor(name, shape, dt).ap()

    ntm_d = dscr("ntm_d", [T, D])
    w1_d = dscr("w1_d", [DC, 128, T])
    w2_d = dscr("w2_d", [DC, 128, T])
    r_d = dscr("r_d", [T, D], F32)
    xmid_d = dscr("xmid_d", [T, D], F32)
    hpre_d = dscr("hpre_d", [IC, 128, T])
    qwg_d = dscr("qwg_d", [IC, DC, 128, 128])
    qwu_d = dscr("qwu_d", [IC, DC, 128, 128])
    sdrow_d = dscr("sdrow_d", [DC, 128], F32)
    scr8_d = dscr("scr8_d", [8], F32)
    cc1i = nc.dram_tensor("cc1i", [128, 8], F32)
    cc1o = nc.dram_tensor("cc1o", [128, 8], F32)
    cc2i = nc.dram_tensor("cc2i", [128, 8], F32)
    cc2o = nc.dram_tensor("cc2o", [128, 8], F32)

    tap_outs = {}
    for t in taps:
        shp, dt = {
            "ntm": ([T, D], BF16), "nfm": ([DC * 128, T], BF16),
            "w1": ([DC, 128, T], BF16), "w2": ([DC, 128, T], BF16),
            "gt": ([T, D], BF16), "r": ([T, D], F32), "xmid": ([T, D], F32),
            "mfm": ([DC * 128, T], BF16), "xq": ([DC * 128, T], BF16),
            "hpre": ([IC, 128, T], BF16), "qwdT": ([IC * 128, D], BF16),
        }[t]
        tap_outs[t] = nc.dram_tensor("tap_" + t, shp, dt, kind="ExternalOutput").ap()

    class _Stop(Exception):
        pass

    es = contextlib.ExitStack()
    with es:
        sb = lambda name, shape, dt: es.enter_context(nc.sbuf_tensor(name, shape, dt))
        AWIDE = max(16 * T, 32768)
        AR1 = sb("AR1", [128, AWIDE], BF16)
        AR2 = sb("AR2", [128, AWIDE], BF16)
        AR3 = sb("AR3", [128, 8 * T], BF16)
        CH = sb("CH", [128, 4 * D], F32)
        WS = sb("WS", [128, 2, 8, 256], BF16)
        g_bc = sb("g_bc", [128, D], F32)
        b_bc = sb("b_bc", [128, D], F32)
        sdh_bc = sb("sdh_bc", [128, D], F32)
        ident_b = sb("ident_b", [128, 128], BF16)
        ident_f = sb("ident_f", [128, 128], F32)
        trimask = sb("trimask", [128, 128], BF16)
        ones_b = sb("ones_b", [128, 128], BF16)
        onesrow_b = sb("onesrow_b", [1, 128], BF16)
        st6 = sb("st6", [128, 12], F32)
        mv2 = sb("mv2", [128, 2], F32)
        tmp1 = sb("tmp1", [128, 1], F32)
        tmp2 = sb("tmp2", [128, 1], F32)
        tmp3 = sb("tmp3", [128, 1], F32)
        tmpr = sb("tmpr", [128, 1], F32)
        mu_all = sb("mu_all", [128, TC], F32)
        rsx_all = sb("rsx_all", [128, TC], F32)
        nmrs_all = sb("nmrs_all", [128, TC], F32)
        rsc_all = sb("rsc_all", [128, TC], F32)
        nmrsc_all = sb("nmrsc_all", [128, TC], F32)
        mu1_all = sb("mu1_all", [128, TC], F32)
        rs1_all = sb("rs1_all", [128, TC], F32)
        nmrs1_all = sb("nmrs1_all", [128, TC], F32)
        bvec32 = sb("bvec32", [128, 32], F32)
        bvec8 = sb("bvec8", [128, 8], F32)
        brow = sb("brow", [32, 128], F32)
        fmb2row = sb("fmb2row", [1, D], BF16)
        colmax = sb("colmax", [128, TC], F32)
        hmax_all = sb("hmax_all", [128, IC], F32)
        cl8 = sb("cl8", [128, 8], F32)
        ggrow = sb("ggrow", [1, 128], F32)
        gsc = sb("gsc", [1, 8], F32)
        clipsb = sb("clipsb", [1, 2], F32)
        sx_vec = sb("sx_vec", [128, 1], F32)
        rsx_q = sb("rsx_q", [128, 1], F32)
        sh_vec = sb("sh_vec", [128, 1], F32)
        rsh_q = sb("rsh_q", [128, 1], F32)
        sg_all = sb("sg_all", [128, IC], F32)
        su_all = sb("su_all", [128, IC], F32)
        sd_all = sb("sd_all", [128, DC], F32)
        nchunk = sb("nchunk", [128, D], BF16)
        evb = sb("evb", [128, 2, 512], BF16)

        PS_A = es.enter_context(nc.psum_tensor("PS_A", [128, 2048], F32))
        PS_B = es.enter_context(nc.psum_tensor("PS_B", [128, 1536], F32))
        PS_TR = es.enter_context(nc.psum_tensor("PS_TR", [128, 8, 128], BF16))

        c = Ctx(nc, es)
        def ckstop(n):
            if stop <= n:
                raise _Stop()
        dS = c.new_sem("d_sync_g")
        dG = c.new_sem("d_gpsimd_g")
        dDQ = c.new_sem("d_sync_dq")

        xs = Stream(c, "xs", Q_LOAD, 2)
        ws = Stream(c, "ws", Q_CAST, 2)
        wrs = Stream(c, "wrs", Q_CAST, 2)
        qws = Stream(c, "qws", Q_LOAD, 4)
        ntm_st = StoreSlots(c, "ntm", Q_STORE, 1)
        wout_st = StoreSlots(c, "wout", Q_STORE, 2)
        r_st = StoreSlots(c, "rst", Q_STORE, 2)
        xmid_st = StoreSlots(c, "xmst", Q_STORE, 2)
        q_st = StoreSlots(c, "qst", Q_STORE, 2)
        h_st = StoreSlots(c, "hst", Q_STORE, 2)
        o_st = StoreSlots(c, "ost", Q_STORE, 1)
        sd_st = StoreSlots(c, "sdst", Q_STORE, 1)

        AR1b = AR1[:, :]
        AR2b = AR2[:, :]
        AR3b = AR3[:, :]
        A_buf = AR1b[:, :16 * T].rearrange("p (a t) -> p a t", a=32)
        w2res = AR2b[:, :32 * D].rearrange("p (a t) -> p a t", a=32)
        n_fm = AR3b.rearrange("p (a t) -> p a t", a=DC)
        m_fm = n_fm
        xq_fm = n_fm
        W1res = AR1b[:, :DC * T].rearrange("p (a t) -> p a t", a=DC)
        W2res = AR1b[:, DC * T:2 * DC * T].rearrange("p (a t) -> p a t", a=DC)
        ntm_res = AR2b[:, :TC * D].rearrange("p (a t) -> p a t", a=TC)
        g_t = AR2b[:, TC * D:2 * TC * D].rearrange("p (a t) -> p a t", a=TC)
        W3Tb = CH[:, :].bitcast(BF16).rearrange("p (a t) -> p a t", a=16)
        CHs = [CH[:, j * D:(j + 1) * D] for j in range(4)]
        CHbf = [CH[:, j * D:(j + 1) * D].bitcast(BF16) for j in range(4)]
        AR1f = AR1b.bitcast(F32)
        qch = CH[:, 2 * D:3 * D]
        qnchunk = nchunk
        qtile = CH[:, 3 * D:4 * D].bitcast(BF16).rearrange("p (s a t) -> p s a t", s=2, a=DC)
        dq_f32 = AR1f[:, :I]
        dq_q = AR1b[:, 2 * I:3 * I]
        qwdT = AR2b[:, :IC * D].rearrange("p (a t) -> p a t", a=IC)
        hbuf_a = AR1f[:, :T]
        hbuf_b = AR1f[:, T:2 * T]
        hbuf_c = AR1b[:, 8 * T:10 * T].rearrange("p (s t) -> p s t", s=2)
        def wst_dst(slot):
            return WS[:, slot // 2, :, (slot % 2) * 128:(slot % 2) * 128 + 128]

        def wst_tile(slot, a):
            return WS[:, slot // 2, a, (slot % 2) * 128:(slot % 2) * 128 + 128]
        hq_a = AR1b[:, :16 * T].rearrange("p (a t) -> p a t", a=16)
        hq_b = AR3b[:, :6 * T].rearrange("p (a t) -> p a t", a=6)

        def hq_row(ic):
            return hq_a[:, ic, :] if ic < 16 else hq_b[:, ic - 16, :]

        def hq_tile(ic, tcc):
            r = hq_row(ic)
            return r[:, tcc * 128:(tcc + 1) * 128]

        psA = [PS_A[:, j * 512:(j + 1) * 512] for j in range(4)]
        psB = [PS_B[:, j * 512:(j + 1) * 512] for j in range(3)]
        psT2 = [PS_A[:, 0:1024], PS_A[:, 1024:2048]]
        psTR = [PS_TR[:, j, :] for j in range(8)]
        psTRf = PS_TR[:, :, :].rearrange("p a t -> p (a t)").bitcast(F32)

        try:
            c.dma(Q_LOAD, g_bc[:, :], lng_h.partition_broadcast(128), dS)
            c.dma(Q_LOAD, b_bc[:, :], lnb_h.partition_broadcast(128), dS)
            c.dma(Q_LOAD, clipsb[0:1, 0:1], clipx_h.unsqueeze(0), dS)
            c.dma(Q_LOAD, clipsb[0:1, 1:2], clipd_h.unsqueeze(0), dS)
            c.dma(Q_CAST, fmb2row[0:1, :], fmb2_h.unsqueeze(0), dG)
            c.op("vector", lambda e: e.memset(ones_b[:, :], 1.0), inc=False)
            tm = c.op("vector", lambda e: e.memset(onesrow_b[0:1, :], 1.0))
            c.wp("gpsimd", "vector", tm)
            c.op("gpsimd", lambda e: e.affine_select(
                out=ident_b[:, :], in_=ones_b[:, :], pattern=[[1, 128]],
                compare_op=ALU.is_equal, fill=0.0, base=0, channel_multiplier=-1), inc=False)
            tg = c.op("gpsimd", lambda e: e.affine_select(
                out=trimask[:, :], in_=ones_b[:, :], pattern=[[1, 128]],
                compare_op=ALU.is_ge, fill=0.0, base=0, channel_multiplier=-1))
            c.wp("vector", "gpsimd", tg)
            c.op("vector", lambda e: e.tensor_copy(ident_f[:, :], ident_b[:, :]), inc=False)
            c.barrier()

            pcvec_state = {"last_tr": None, "last_ev": None}

            def load_pcvec(b_ap, n, dst):
                if pcvec_state["last_tr"] is not None:
                    c.wp(Q_LOAD, "tensor", pcvec_state["last_tr"])
                t1 = c.dma(Q_LOAD, brow[0:n, :], b_ap.rearrange("(c p) -> c p", p=128), dS)
                jt = c.qjoin(Q_LOAD)
                c.wp("tensor", Q_LOAD, jt)
                if pcvec_state["last_ev"] is not None:
                    c.wp("tensor", "vector", pcvec_state["last_ev"])
                tt = c.op("tensor", lambda e, n=n: e.transpose(PS_B[:, 0:n], brow[0:n, 0:128], ident_f[0:n, 0:n]))
                pcvec_state["last_tr"] = tt
                c.wp("vector", "tensor", tt)
                ev = c.op("vector", lambda e, n=n, d=dst: e.tensor_copy(d[:, 0:n], PS_B[:, 0:n]))
                pcvec_state["last_ev"] = ev
                return ev

            def ln_stats(buf, rs_col, nmrs_col, mu_col=None):
                c.op("vector", lambda e, b=buf: e.bn_stats(st6[:, 0:6], b[:, 0:512]), inc=False)
                c.op("vector", lambda e, b=buf: e.bn_stats(st6[:, 6:12], b[:, 512:1024]), inc=False)
                c.op("vector", lambda e: e.bn_aggr(mv2[:, 0:2], st6[:, 0:12]), inc=False)
                if mu_col is not None:
                    c.op("vector", lambda e, m=mu_col: e.tensor_copy(m, mv2[:, 0:1]), inc=False)
                tv = c.op("vector", lambda e: e.tensor_scalar(tmp1[:, 0:1], mv2[:, 1:2], EPS, None, ALU.add))
                c.wp("scalar", "vector", tv)
                ts_ = c.op("scalar", lambda e: e.activation(tmp2[:, 0:1], tmp1[:, 0:1], AF.Sqrt))
                c.wp("vector", "scalar", ts_)
                c.op("vector", lambda e, r=rs_col: e.reciprocal(r, tmp2[:, 0:1]), inc=False)
                c.op("vector", lambda e, r=rs_col: e.tensor_tensor(tmp3[:, 0:1], mv2[:, 0:1], r, ALU.mult), inc=False)
                return c.op("vector", lambda e, nm=nmrs_col: e.tensor_scalar(nm, tmp3[:, 0:1], -1.0, None, ALU.mult))

            def ln_stats2(buf):
                c.op("vector", lambda e, b=buf: e.bn_stats(st6[:, 0:6], b[:, 0:512]), inc=False)
                c.op("vector", lambda e, b=buf: e.bn_stats(st6[:, 6:12], b[:, 512:1024]), inc=False)
                c.op("vector", lambda e: e.bn_aggr(mv2[:, 0:2], st6[:, 0:12]), inc=False)
                tv = c.op("vector", lambda e: e.tensor_scalar(tmp1[:, 0:1], mv2[:, 1:2], EPS, None, ALU.add))
                c.wp("scalar", "vector", tv)
                ts_ = c.op("scalar", lambda e: e.activation(tmp2[:, 0:1], tmp1[:, 0:1], AF.Sqrt))
                c.wp("vector", "scalar", ts_)
                c.op("vector", lambda e: e.reciprocal(tmpr[:, 0:1], tmp2[:, 0:1]), inc=False)
                c.op("vector", lambda e: e.tensor_tensor(tmp3[:, 0:1], mv2[:, 0:1], tmpr[:, 0:1], ALU.mult), inc=False)
                return c.op("vector", lambda e: e.tensor_scalar(tmp3[:, 0:1], tmp3[:, 0:1], -1.0, None, ALU.mult))

            p0_free = {}
            last_tr = None
            for tc in range(TC):
                xbuf = CHs[tc % 2]
                ssem, stgt = xs.fill(xbuf, x_h[tc * 128:(tc + 1) * 128, :], free=p0_free.get(tc - 2))
                c.wait("vector", ssem, stgt)
                tvs = ln_stats(xbuf, rsx_all[:, tc:tc + 1], nmrs_all[:, tc:tc + 1], mu_all[:, tc:tc + 1])
                c.op("vector", lambda e, t=tc: e.tensor_tensor(tmp1[:, 0:1], mv2[:, 1:2], rsx_all[:, t:t + 1], ALU.mult), inc=False)
                c.op("vector", lambda e, t=tc: e.tensor_tensor(tmp1[:, 0:1], tmp1[:, 0:1], rsx_all[:, t:t + 1], ALU.mult), inc=False)
                tv = c.op("vector", lambda e: e.tensor_scalar(tmp1[:, 0:1], tmp1[:, 0:1], EPS, None, ALU.add))
                c.wp("scalar", "vector", tv)
                ts_ = c.op("scalar", lambda e: e.activation(tmp2[:, 0:1], tmp1[:, 0:1], AF.Sqrt))
                c.wp("vector", "scalar", ts_)
                c.op("vector", lambda e: e.reciprocal(tmp3[:, 0:1], tmp2[:, 0:1]), inc=False)
                tv = c.op("vector", lambda e, t=tc: e.tensor_tensor(rsc_all[:, t:t + 1], rsx_all[:, t:t + 1], tmp3[:, 0:1], ALU.mult))
                c.op("vector", lambda e, t=tc: e.tensor_tensor(tmp3[:, 0:1], mu_all[:, t:t + 1], rsc_all[:, t:t + 1], ALU.mult), inc=False)
                tv = c.op("vector", lambda e, t=tc: e.tensor_scalar(nmrsc_all[:, t:t + 1], tmp3[:, 0:1], -1.0, None, ALU.mult))
                c.wp("scalar", "vector", tv)
                ts_ = c.op("scalar", lambda e, xb=xbuf, t=tc: e.activation(
                    CHs[2][:, :], xb, AF.Identity,
                    bias=nmrsc_all[:, t:t + 1], scale=rsc_all[:, t:t + 1]))
                p0_free[tc] = ("p_scalar", ts_)
                c.wp("vector", "scalar", ts_)
                if last_tr is not None:
                    c.wp("vector", "tensor", last_tr)
                    c.wait("vector", ntm_st.sl[0], c.cnt[ntm_st.sl[0]])
                c.op("vector", lambda e: e.tensor_tensor(CHs[2][:, :], CHs[2][:, :], g_bc[:, :], ALU.mult), inc=False)
                tv_n = c.op("vector", lambda e: e.tensor_tensor(nchunk[:, :], CHs[2][:, :], b_bc[:, :], ALU.add))
                ntm_st.store(ntm_d[tc * 128:(tc + 1) * 128, :], nchunk[:, :], "vector", tv_n)
                c.wp("tensor", "vector", tv_n)
                for dc in range(DC):
                    tt = c.op("tensor", lambda e, dcc=dc: e.transpose(
                        psTR[dcc], nchunk[:, dcc * 128:(dcc + 1) * 128], ident_b[:, :]), inc=(dc == DC - 1))
                    if dc == DC - 1:
                        last_tr = tt
                c.wp("vector", "tensor", last_tr)
                for dc in range(DC):
                    c.op("vector", lambda e, dcc=dc, t=tc: e.tensor_copy(
                        n_fm[:, dcc, t * 128:(t + 1) * 128], psTR[dcc]), inc=False)
            c.barrier()
            if "ntm" in tap_outs:
                c.dma(Q_LOAD, tap_outs["ntm"], ntm_d, dS); c.qjoin(Q_LOAD); c.barrier()
            if "nfm" in tap_outs:
                for dcc in range(DC):
                    c.dma(Q_LOAD, tap_outs["nfm"][dcc * 128:(dcc + 1) * 128, :], n_fm[:, dcc, :], dS)
                c.qjoin(Q_LOAD); c.barrier()

            ckstop(1)

            def mlp_phase(w1_ap, b1_ap, w2_ap, b2_ap, src_fm, wout_d, fm_mode=False,
                          filler=None):
                tvb = load_pcvec(b1_ap, 32, bvec32)
                if not fm_mode:
                    tvb = load_pcvec(b2_ap, DC, bvec8)
                c.barrier()
                wr_tgts = {}
                for half in range(2):
                    toff = half * TH
                    ev_done = {}
                    slot_last_use = [None, None]
                    for sc in range(DI // 256):
                        if filler is not None:
                            filler()
                        src = w1_ap.rearrange("(dc p) m -> p dc m", p=128)[:, :, sc * 256:(sc + 1) * 256]
                        wsl = sc % 2
                        ssem, stgt = ws.fill(WS[:, wsl, :, :], src, free=slot_last_use[wsl])
                        if half == 0 and sc % 2 == 0 and sc // 2 < DI // 512:
                            rsc = sc // 2
                            wsrc = w2_ap.rearrange("(a p) d -> p a d", p=128)[:, rsc * 4:(rsc + 1) * 4, :]
                            s2, t2 = wrs.fill(w2res[:, rsc * 4:(rsc + 1) * 4, :], wsrc)
                            wr_tgts[s2] = t2
                        for k in range(2):
                            dic = sc * 2 + k
                            pair = dic % 2
                            c.wait("tensor", ssem, stgt)
                            if dic >= 2:
                                c.wp("tensor", "scalar", ev_done[dic - 2])
                            last = None
                            for tsp in range(NTS):
                                bank = psA[pair * 2 + tsp]
                                for dc in range(DC):
                                    isl = (tsp == NTS - 1 and dc == DC - 1)
                                    def mmf(e, dcc=dc, tspp=tsp, kk=k, bank=bank, wsl=wsl, toff=toff):
                                        return e.matmul(
                                            bank, WS[:, wsl, dcc, kk * 128:(kk + 1) * 128],
                                            src_fm[:, dcc, toff + tspp * 512: toff + tspp * 512 + 512],
                                            start=(dcc == 0), stop=(dcc == DC - 1))
                                    last = c.op("tensor", mmf, inc=isl)
                            if k == 1:
                                slot_last_use[wsl] = ("p_tensor", last)
                            c.wp("scalar", "tensor", last)
                            for tsp in range(NTS):
                                bank = psA[pair * 2 + tsp]
                                ev = c.op("scalar", lambda e, dicc=dic, tspp=tsp, bank=bank:
                                          e.activation(A_buf[:, dicc, tspp * 512: tspp * 512 + 512],
                                                       bank, AF.Gelu, bias=bvec32[:, dicc:dicc + 1]),
                                          inc=(tsp == NTS - 1))
                            ev_done[dic] = ev
                    c.wp("tensor", "scalar", ev_done[DIC - 1])
                    for ssem, stgt in wr_tgts.items():
                        c.wait("tensor", ssem, stgt)
                    if not fm_mode:
                        ev_w = {}
                        st_prev = {}
                        last_w = None
                        for dc in range(DC):
                            for tsp in range(NTS):
                                key = dc * NTS + tsp
                                bank = psB[key % 2]
                                if key >= 2:
                                    c.wp("tensor", "scalar", ev_w[key - 2])
                                for dic in range(DIC):
                                    def mmf(e, dicc=dic, dcc=dc, tspp=tsp, bank=bank):
                                        return e.matmul(
                                            bank, w2res[:, dicc, dcc * 128:(dcc + 1) * 128],
                                            A_buf[:, dicc, tspp * 512: tspp * 512 + 512],
                                            start=(dicc == 0), stop=(dicc == DIC - 1))
                                    last_w = c.op("tensor", mmf, inc=(dic == DIC - 1))
                                c.wp("scalar", "tensor", last_w)
                                esl = key % 2
                                if key >= 2:
                                    sem, tgt = st_prev[key - 2]
                                    c.wait("scalar", sem, tgt)
                                ev = c.op("scalar", lambda e, bank=bank, esl=esl, dcc=dc:
                                          e.activation(evb[:, esl, :], bank, AF.Identity,
                                                       bias=bvec8[:, dcc:dcc + 1]))
                                ev_w[key] = ev
                                st_prev[key] = wout_st.store(
                                    wout_d[dc][:, toff + tsp * 512: toff + tsp * 512 + 512],
                                    evb[:, esl, :], "scalar", ev)
                        lw_final = last_w
                    else:
                        fev = {}
                        rfree = {}
                        for tk in range(TH // 128):
                            tcg = half * (TH // 128) + tk
                            bank2 = psT2[tk % 2]
                            if tk >= 2:
                                c.wp("tensor", "vector", fev[tk - 2])
                            last = None
                            for dic in range(DIC):
                                for nsp in range(2):
                                    def mmf(e, dicc=dic, nspp=nsp, bank2=bank2, tkk=tk):
                                        return e.matmul(
                                            bank2[:, nspp * 512:(nspp + 1) * 512],
                                            A_buf[:, dicc, tkk * 128:(tkk + 1) * 128],
                                            w2res[:, dicc, nspp * 512:(nspp + 1) * 512],
                                            start=(dicc == 0), stop=False)
                                    c.op("tensor", mmf, inc=False)
                            for nsp in range(2):
                                def mmb(e, nspp=nsp, bank2=bank2):
                                    return e.matmul(
                                        bank2[:, nspp * 512:(nspp + 1) * 512],
                                        onesrow_b[0:1, :], fmb2row[0:1, nspp * 512:(nspp + 1) * 512],
                                        start=False, stop=True)
                                last = c.op("tensor", mmb, inc=(nsp == 1))
                            rsl = CHs[tk % 2]
                            ssem, stgt = xs.fill(rsl, r_d[tcg * 128:(tcg + 1) * 128, :], free=rfree.get(tk - 2))
                            c.wp("vector", "tensor", last)
                            c.wait("vector", ssem, stgt)
                            if tk >= 1:
                                sem, tgt = xm_prev[tk - 1]
                                c.wait("vector", sem, tgt)
                            tv = c.op("vector", lambda e, bank2=bank2, rb=rsl: e.tensor_tensor(
                                CHs[2][:, :], bank2[:, 0:1024], rb, ALU.add))
                            fev[tk] = tv
                            rfree[tk] = ("p_vector", tv)
                            xm_prev[tk] = xmid_st.store(
                                xmid_d[tcg * 128:(tcg + 1) * 128, :], CHs[2][:, :], "vector", tv)
                            tvs = ln_stats(CHs[2][:, :], rs1_all[:, tcg:tcg + 1],
                                           nmrs1_all[:, tcg:tcg + 1], mu1_all[:, tcg:tcg + 1])
                            c.wp("scalar", "vector", tvs)
                            ts_ = c.op("scalar", lambda e, t=tcg: e.activation(
                                CHs[3][:, :], CHs[2][:, :], AF.Identity,
                                bias=nmrs1_all[:, t:t + 1], scale=rs1_all[:, t:t + 1]))
                            c.wp("vector", "scalar", ts_)
                            c.op("vector", lambda e, t=tcg: e.tensor_reduce(
                                colmax[:, t:t + 1], CHs[3][:, :], axis=AX.X, op=ALU.max,
                                apply_absolute_value=True), inc=False)
                        lw_final = last
                    c.barrier()

            xm_prev = {}

            qstate = {"i": 0, "free": {}, "qprev": {}}
            qplan = [(wg_h, qwg_d, sg_all, ic) for ic in range(IC)] + \
                    [(wu_h, qwu_d, su_all, ic) for ic in range(IC)]

            def quant_filler():
                nq = qstate["i"]
                if nq >= len(qplan):
                    return
                qstate["i"] += 1
                w_ap, dst, sall, ic = qplan[nq]
                wbuf = CHs[nq % 2]
                ssem, stgt = xs.fill(wbuf, w_ap[ic * 128:(ic + 1) * 128, :],
                                     free=qstate["free"].get(nq - 2))
                c.wait("vector", ssem, stgt)
                c.op("vector", lambda e, wb=wbuf: e.tensor_reduce(
                    tmp1[:, 0:1], wb, axis=AX.X, op=ALU.max, apply_absolute_value=True), inc=False)
                c.op("vector", lambda e, sa=sall, icc=ic: e.tensor_scalar(
                    sa[:, icc:icc + 1], tmp1[:, 0:1], 1.0 / QMAX, None, ALU.mult), inc=False)
                c.op("vector", lambda e, sa=sall, icc=ic: e.reciprocal(tmp2[:, 0:1], sa[:, icc:icc + 1]), inc=False)
                c.op("vector", lambda e, wb=wbuf: e.tensor_scalar(
                    qch[:, :], wb, tmp2[:, 0:1], RND, ALU.mult, ALU.add), inc=False)
                tv_q = c.op("vector", lambda e: e.tensor_scalar(
                    qnchunk[:, :], qch[:, :], -RND, None, ALU.add))
                qstate["free"][nq] = ("p_vector", tv_q)
                c.wp("tensor", "vector", tv_q)
                ltt = None
                for dc in range(DC):
                    ltt = c.op("tensor", lambda e, dcc=dc: e.transpose(
                        psTR[dcc], qnchunk[:, dcc * 128:(dcc + 1) * 128], ident_b[:, :]), inc=(dc == DC - 1))
                c.wp("vector", "tensor", ltt)
                if nq >= 2:
                    sem, tgt = qstate["qprev"][nq - 2]
                    c.wait("vector", sem, tgt)
                tv = None
                for dc in range(DC):
                    tv = c.op("vector", lambda e, dcc=dc, qs=nq % 2: e.tensor_copy(
                        qtile[:, qs, dcc, :], psTR[dcc]), inc=(dc == DC - 1))
                qstate["qprev"][nq] = q_st.store(dst[ic], qtile[:, nq % 2, :, :], "vector", tv)

            mlp_phase(h1w1_h, h1b1_h, h1w2_h, h1b2_h, n_fm, w1_d, filler=quant_filler)
            if "w1" in tap_outs:
                c.dma(Q_LOAD, tap_outs["w1"], w1_d, dS); c.qjoin(Q_LOAD); c.barrier()
            ckstop(2)
            mlp_phase(h2w1_h, h2b1_h, h2w2_h, h2b2_h, n_fm, w2_d, filler=quant_filler)
            if "w2" in tap_outs:
                c.dma(Q_LOAD, tap_outs["w2"], w2_d, dS); c.qjoin(Q_LOAD); c.barrier()

            ckstop(3)
            for dcc in range(DC):
                c.dma(Q_LOAD, W1res[:, dcc, :], w1_d[dcc], dS)
                c.dma(Q_LOAD, W2res[:, dcc, :], w2_d[dcc], dS)
            for tcc in range(TC):
                c.dma(Q_LOAD, ntm_res[:, tcc, :],
                      ntm_d.rearrange("(a p) d -> p a d", p=128)[:, tcc, :], dS)
            jt = c.qjoin(Q_LOAD)
            for e_ in ("tensor", "vector", "scalar"):
                c.wp(e_, Q_LOAD, jt)
            for zs in range(ZS):
                smax = 4 * zs + 3
                sev = {}
                for sc in range(smax + 1):
                    bank = psB[sc % 2]
                    if sc >= 2:
                        c.wp("tensor", "vector", sev[sc - 2])
                    last = None
                    for dc in range(DC):
                        def mmf(e, dcc=dc, scc=sc, zss=zs, bank=bank):
                            return e.matmul(bank, W2res[:, dcc, scc * 128:(scc + 1) * 128],
                                            W1res[:, dcc, zss * 512:zss * 512 + 512],
                                            start=(dcc == 0), stop=(dcc == DC - 1))
                        last = c.op("tensor", mmf, inc=(dc == DC - 1))
                    c.wp("vector", "tensor", last)
                    if sc < 4 * zs:
                        tv = c.op("vector", lambda e, scc=sc, bank=bank: e.tensor_copy(W3Tb[:, scc, :], bank))
                    else:
                        tv = None
                        for j in range(4):
                            zc = zs * 4 + j
                            if zc < sc:
                                continue
                            if zc == sc:
                                tv = c.op("vector", lambda e, scc=sc, jj=j, bank=bank: e.tensor_tensor(
                                    W3Tb[:, scc, jj * 128:(jj + 1) * 128],
                                    bank[:, jj * 128:(jj + 1) * 128], trimask[:, :], ALU.mult))
                            else:
                                tv = c.op("vector", lambda e, scc=sc, jj=j, bank=bank: e.tensor_copy(
                                    W3Tb[:, scc, jj * 128:(jj + 1) * 128],
                                    bank[:, jj * 128:(jj + 1) * 128]))
                    sev[sc] = tv
                gev = {}
                for j in range(4):
                    zc = zs * 4 + j
                    bank2 = psT2[j % 2]
                    if j >= 2:
                        c.wp("tensor", "scalar", gev[j - 2])
                    c.wp("tensor", "vector", sev[min(zc, smax)])
                    last = None
                    for sc in range(zc + 1):
                        for nsp in range(2):
                            def mmf(e, scc=sc, jj=j, nspp=nsp, bank2=bank2, zcc=zc):
                                return e.matmul(
                                    bank2[:, nspp * 512:(nspp + 1) * 512],
                                    W3Tb[:, scc, jj * 128:(jj + 1) * 128],
                                    ntm_res[:, scc, nspp * 512:(nspp + 1) * 512],
                                    start=(scc == 0), stop=(scc == zcc))
                            last = c.op("tensor", mmf, inc=(sc == zc and nsp == 1))
                    c.wp("scalar", "tensor", last)
                    gev[j] = c.op("scalar", lambda e, zcc=zc, bank2=bank2: e.activation(
                        g_t[:, zcc, :], bank2[:, 0:1024], AF.Gelu))
                c.barrier()
            if "gt" in tap_outs:
                for tcc in range(TC):
                    c.dma(Q_LOAD, tap_outs["gt"][tcc * 128:(tcc + 1) * 128, :], g_t[:, tcc, :], dS)
                c.qjoin(Q_LOAD); c.barrier()

            ckstop(4)
            r_prev = {}
            last_tr = None
            tvm_prev = None
            for tc in range(TC):
                xbuf = CHs[tc % 2]
                ssem, stgt = xs.fill(xbuf, x_h[tc * 128:(tc + 1) * 128, :], free=r_prev.get(tc - 2))
                c.wait("scalar", ssem, stgt)
                if tvm_prev is not None:
                    c.wp("scalar", "vector", tvm_prev)
                ts_y = c.op("scalar", lambda e, xb=xbuf, t=tc: e.activation(
                    CHs[2][:, :], xb, AF.Identity,
                    bias=nmrs_all[:, t:t + 1], scale=rsx_all[:, t:t + 1]))
                c.wp("vector", "scalar", ts_y)
                tv_r = c.op("vector", lambda e, xb=xbuf: e.tensor_tensor(xb, xb, CHs[2][:, :], ALU.add))
                r_prev[tc] = r_st.store(r_d[tc * 128:(tc + 1) * 128, :], xbuf, "vector", tv_r)
                tvs = ln_stats2(g_t[:, tc, :])
                c.wp("scalar", "vector", tvs)
                ts_ = c.op("scalar", lambda e, t=tc: e.activation(
                    CHs[3][:, :], g_t[:, t, :], AF.Identity, bias=tmp3[:, 0:1], scale=tmpr[:, 0:1]))
                c.wp("vector", "scalar", ts_)
                c.op("vector", lambda e: e.tensor_tensor(CHs[3][:, :], CHs[3][:, :], g_bc[:, :], ALU.mult), inc=False)
                c.op("vector", lambda e: e.tensor_tensor(CHs[3][:, :], CHs[3][:, :], b_bc[:, :], ALU.add), inc=False)
                c.op("vector", lambda e: e.tensor_tensor(CHs[3][:, :], CHs[3][:, :], CHs[2][:, :], ALU.add), inc=False)
                tvs = ln_stats2(CHs[3][:, :])
                c.wp("scalar", "vector", tvs)
                ts_ = c.op("scalar", lambda e: e.activation(
                    CHs[2][:, :], CHs[3][:, :], AF.Identity, bias=tmp3[:, 0:1], scale=tmpr[:, 0:1]))
                c.wp("vector", "scalar", ts_)
                if last_tr is not None:
                    c.wp("vector", "tensor", last_tr)
                c.op("vector", lambda e: e.tensor_tensor(CHs[2][:, :], CHs[2][:, :], g_bc[:, :], ALU.mult), inc=False)
                tv_m = c.op("vector", lambda e: e.tensor_tensor(nchunk[:, :], CHs[2][:, :], b_bc[:, :], ALU.add))
                tvm_prev = tv_m
                c.wp("tensor", "vector", tv_m)
                for dc in range(DC):
                    last_tr = c.op("tensor", lambda e, dcc=dc: e.transpose(
                        psTR[dcc], nchunk[:, dcc * 128:(dcc + 1) * 128], ident_b[:, :]), inc=(dc == DC - 1))
                c.wp("vector", "tensor", last_tr)
                for dc in range(DC):
                    c.op("vector", lambda e, dcc=dc, t=tc: e.tensor_copy(
                        m_fm[:, dcc, t * 128:(t + 1) * 128], psTR[dcc]), inc=False)
            c.barrier()
            if "r" in tap_outs:
                c.dma(Q_LOAD, tap_outs["r"], r_d, dS); c.qjoin(Q_LOAD); c.barrier()
            if "mfm" in tap_outs:
                for dcc in range(DC):
                    c.dma(Q_LOAD, tap_outs["mfm"][dcc * 128:(dcc + 1) * 128, :], m_fm[:, dcc, :], dS)
                c.qjoin(Q_LOAD); c.barrier()

            ckstop(5)
            mlp_phase(fmw1_h, fmb1_h, fmw2_h, fmb2_h, m_fm, None, fm_mode=True)
            if "xmid" in tap_outs:
                c.dma(Q_LOAD, tap_outs["xmid"], xmid_d, dS); c.qjoin(Q_LOAD); c.barrier()

            ckstop(6)
            def global_scalar(src_cols, ncols, cci, cco, clip_col, sv, rv, scr_off):
                tv = c.op("vector", lambda e: e.memset(cl8[:, :], 0.0))
                tv = c.op("vector", lambda e, n=ncols, s=src_cols: e.tensor_reduce(
                    cl8[:, 0:1], s, axis=AX.X, op=ALU.max))
                c.wp("gpsimd", "vector", tv)
                c.dma(Q_CAST, cci.ap(), cl8[:, :], dG)
                jt = c.qjoin(Q_CAST)
                ct = c.op("gpsimd", lambda e, ci=cci, co=cco: e.collective_compute(
                    "AllReduce", ALU.max, replica_groups=RG8,
                    ins=[ci.ap().opt()], outs=[co.ap().opt()]))
                c.wait("gpsimd", "p_gpsimd", ct)
                c.dma(Q_CAST, ggrow[0:1, :], cco.ap()[:, 0:1].rearrange("p a -> a p"), dG, slow=True)
                jt = c.qjoin(Q_CAST)
                c.wp("vector", Q_CAST, jt)
                tv = c.op("vector", lambda e: e.tensor_reduce(gsc[0:1, 0:1], ggrow[0:1, :], axis=AX.X, op=ALU.max))
                c.wp("scalar", "vector", tv)
                ts_ = c.op("scalar", lambda e, cc=clip_col: e.activation(gsc[0:1, 1:2], clipsb[0:1, cc:cc + 1], AF.Sigmoid))
                c.wp("vector", "scalar", ts_)
                c.op("vector", lambda e: e.tensor_tensor(gsc[0:1, 2:3], gsc[0:1, 0:1], gsc[0:1, 1:2], ALU.mult), inc=False)
                c.op("vector", lambda e: e.tensor_scalar(gsc[0:1, 2:3], gsc[0:1, 2:3], 1.0 / QMAX, None, ALU.mult), inc=False)
                tv = c.op("vector", lambda e: e.reciprocal(gsc[0:1, 3:4], gsc[0:1, 2:3]))
                c.wp("gpsimd", "vector", tv)
                c.dma(Q_CAST, scr8_d[scr_off:scr_off + 2], gsc[0:1, 2:4], dG)
                jt = c.qjoin(Q_CAST)
                c.dma(Q_CAST, sv[:, 0:1], scr8_d[scr_off:scr_off + 1].partition_broadcast(128), dG)
                c.dma(Q_CAST, rv[:, 0:1], scr8_d[scr_off + 1:scr_off + 2].partition_broadcast(128), dG)
                c.barrier()

            global_scalar(colmax[:, 0:TC], TC, cc1i, cc1o, 0, sx_vec, rsx_q, 0)

            ckstop(7)
            while qstate["i"] < len(qplan):
                quant_filler()
            c.barrier()

            ckstop(8)
            evtr = {}
            for hc in range(DC):
                t1 = c.dma(Q_LOAD, dq_f32[:, :], wd_h[hc * 128:(hc + 1) * 128, :], dDQ)
                c.wait("vector", dDQ, t1)
                c.op("vector", lambda e: e.tensor_reduce(
                    tmp1[:, 0:1], dq_f32[:, :], axis=AX.X, op=ALU.max, apply_absolute_value=True), inc=False)
                c.op("vector", lambda e, h=hc: e.tensor_scalar(
                    sd_all[:, h:h + 1], tmp1[:, 0:1], 1.0 / QMAX, None, ALU.mult), inc=False)
                c.op("vector", lambda e, h=hc: e.reciprocal(tmp2[:, 0:1], sd_all[:, h:h + 1]), inc=False)
                c.op("vector", lambda e: e.tensor_scalar(
                    dq_f32[:, :], dq_f32[:, :], tmp2[:, 0:1], RND, ALU.mult, ALU.add), inc=False)
                tv_q = c.op("vector", lambda e: e.tensor_scalar(
                    dq_q[:, :], dq_f32[:, :], -RND, None, ALU.add))
                c.wp("tensor", "vector", tv_q)
                for g0 in range(0, IC, 8):
                    g1 = min(g0 + 8, IC)
                    if g0 > 0:
                        c.wp("tensor", "vector", evtr[g0 - 1])
                    ltt = None
                    for i2 in range(g0, g1):
                        ltt = c.op("tensor", lambda e, i2=i2: e.transpose(
                            psTR[i2 % 8], dq_q[:, i2 * 128:(i2 + 1) * 128], ident_b[:, :]),
                            inc=(i2 == g1 - 1))
                    c.wp("vector", "tensor", ltt)
                    for i2 in range(g0, g1):
                        evtr[i2] = c.op("vector", lambda e, i2=i2, h=hc: e.tensor_copy(
                            qwdT[:, i2, h * 128:(h + 1) * 128], psTR[i2 % 8]))
                c.barrier()
            ckstop(8.2)
            if "qwdT" in tap_outs:
                for i2 in range(IC):
                    c.dma(Q_LOAD, tap_outs["qwdT"][i2 * 128:(i2 + 1) * 128, :], qwdT[:, i2, :], dS)
                c.qjoin(Q_LOAD); c.barrier()

            tt = c.op("tensor", lambda e: e.transpose(PS_B[0:DC, 0:128], sd_all[:, 0:DC], ident_f[:, :]))
            c.wp("vector", "tensor", tt)
            tv = c.op("vector", lambda e: e.tensor_copy(brow[0:DC, 0:128], PS_B[0:DC, 0:128]))
            ckstop(8.5)
            sd_st.store(sdrow_d, brow[0:DC, 0:128], "vector", tv)
            jt = c.qjoin(Q_STORE)
            ckstop(8.7)
            c.wp(Q_LOAD, Q_STORE, jt)
            c.dma(Q_LOAD, sdh_bc[:, :], sdrow_d.rearrange("a q -> (a q)").partition_broadcast(128), dS)
            c.qjoin(Q_LOAD)
            c.barrier()

            ckstop(9)
            xqfree = {}
            last_tr = None
            xq_ch2_prev = None
            for tc in range(TC):
                xbuf = CHs[tc % 2]
                ssem, stgt = xs.fill(xbuf, xmid_d[tc * 128:(tc + 1) * 128, :], free=xqfree.get(tc - 2))
                c.wait("scalar", ssem, stgt)
                if xq_ch2_prev is not None:
                    c.wp("scalar", "vector", xq_ch2_prev)
                ts_ = c.op("scalar", lambda e, xb=xbuf, t=tc: e.activation(
                    CHs[2][:, :], xb, AF.Identity,
                    bias=nmrs1_all[:, t:t + 1], scale=rs1_all[:, t:t + 1]))
                xqfree[tc] = ("p_scalar", ts_)
                c.wp("vector", "scalar", ts_)
                if last_tr is not None:
                    c.wp("vector", "tensor", last_tr)
                xq_ch2_prev = c.op("vector", lambda e: e.tensor_scalar(
                    CHs[3][:, :], CHs[2][:, :], rsx_q[:, 0:1], RND, ALU.mult, ALU.add))
                c.op("vector", lambda e: e.tensor_scalar(
                    CHs[3][:, :], CHs[3][:, :], -RND, QMAX, ALU.add, ALU.min), inc=False)
                tv_q = c.op("vector", lambda e: e.tensor_scalar(
                    nchunk[:, :], CHs[3][:, :], -(QMAX + 1.0), None, ALU.max))
                c.wp("tensor", "vector", tv_q)
                for dc in range(DC):
                    last_tr = c.op("tensor", lambda e, dcc=dc: e.transpose(
                        psTR[dcc], nchunk[:, dcc * 128:(dcc + 1) * 128], ident_b[:, :]), inc=(dc == DC - 1))
                c.wp("vector", "tensor", last_tr)
                for dc in range(DC):
                    c.op("vector", lambda e, dcc=dc, t=tc: e.tensor_copy(
                        xq_fm[:, dcc, t * 128:(t + 1) * 128], psTR[dcc]), inc=False)
            c.barrier()
            if "xq" in tap_outs:
                for dcc in range(DC):
                    c.dma(Q_LOAD, tap_outs["xq"][dcc * 128:(dcc + 1) * 128, :], xq_fm[:, dcc, :], dS)
                c.qjoin(Q_LOAD); c.barrier()

            ckstop(10)
            c.op("vector", lambda e: e.tensor_scalar(sg_all[:, :], sg_all[:, :], sx_vec[:, 0:1], None, ALU.mult), inc=False)
            tvc0 = c.op("vector", lambda e: e.tensor_scalar(su_all[:, :], su_all[:, :], sx_vec[:, 0:1], None, ALU.mult))
            c.wp("scalar", "vector", tvc0)
            sgev = {}
            hprev = {}
            hprev_tvb = {}
            gu_free = [None, None, None, None]
            for ic in range(IC):
                gslot = 2 * (ic % 2)
                uslot = gslot + 1
                gsem, gtgt = qws.fill(wst_dst(gslot), qwg_d[ic], free=gu_free[gslot])
                usem, utgt = qws.fill(wst_dst(uslot), qwu_d[ic], free=gu_free[uslot])
                c.wait("tensor", gsem, gtgt)
                if ic >= 1:
                    c.wp("tensor", "scalar", sgev[ic - 1][0])
                lastg = None
                for tsp in range(NTT):
                    for hc in range(DC):
                        def mmf(e, tspp=tsp, hcc=hc, gs=gslot):
                            return e.matmul(PS_A[:, tspp * 512:(tspp + 1) * 512],
                                            wst_tile(gs, hcc),
                                            xq_fm[:, hcc, tspp * 512:(tspp + 1) * 512],
                                            start=(hcc == 0), stop=(hcc == DC - 1))
                        lastg = c.op("tensor", mmf, inc=(tsp == NTT - 1 and hc == DC - 1))
                gu_free[gslot] = ("p_tensor", lastg)
                c.wp("scalar", "tensor", lastg)
                if ic >= 1:
                    c.wp("scalar", "vector", hprev_tvb[ic - 1])
                ev_g = c.op("scalar", lambda e, icc=ic: e.activation(
                    hbuf_a[:, :], PS_A[:, 0:T], AF.Silu, scale=sg_all[:, icc:icc + 1]))
                c.wait("tensor", usem, utgt)
                if ic >= 1:
                    c.wp("tensor", "scalar", sgev[ic - 1][1])
                lastu = None
                for tsp in range(NTT):
                    bank = psB[tsp] if tsp < 3 else psTRf
                    for hc in range(DC):
                        def mmf(e, tspp=tsp, hcc=hc, us=uslot, bank=bank):
                            return e.matmul(bank, wst_tile(us, hcc),
                                            xq_fm[:, hcc, tspp * 512:(tspp + 1) * 512],
                                            start=(hcc == 0), stop=(hcc == DC - 1))
                        lastu = c.op("tensor", mmf, inc=(tsp == NTT - 1 and hc == DC - 1))
                gu_free[uslot] = ("p_tensor", lastu)
                c.wp("scalar", "tensor", lastu)
                if NTT <= 3:
                    ev_u = c.op("scalar", lambda e, icc=ic: e.activation(
                        hbuf_b[:, :], PS_B[:, 0:T], AF.Identity, scale=su_all[:, icc:icc + 1]))
                else:
                    c.op("scalar", lambda e, icc=ic: e.activation(
                        hbuf_b[:, 0:1536], PS_B[:, 0:1536], AF.Identity, scale=su_all[:, icc:icc + 1]), inc=False)
                    ev_u = c.op("scalar", lambda e, icc=ic: e.activation(
                        hbuf_b[:, 1536:2048], psTRf, AF.Identity, scale=su_all[:, icc:icc + 1]))
                sgev[ic] = (ev_g, ev_u)
                c.wp("vector", "scalar", ev_u)
                c.op("vector", lambda e: e.tensor_tensor(hbuf_a[:, :], hbuf_a[:, :], hbuf_b[:, :], ALU.mult), inc=False)
                c.op("vector", lambda e, icc=ic: e.tensor_reduce(
                    hmax_all[:, icc:icc + 1], hbuf_a[:, :], axis=AX.X,
                    op=ALU.max, apply_absolute_value=True), inc=False)
                if ic >= 2:
                    sem, tgt = hprev[ic - 2]
                    c.wait("vector", sem, tgt)
                tvb = c.op("vector", lambda e, icc=ic: e.tensor_copy(hbuf_c[:, icc % 2, :], hbuf_a[:, :]))
                hprev_tvb[ic] = tvb
                hprev[ic] = h_st.store(hpre_d[ic], hbuf_c[:, ic % 2, :], "vector", tvb)
            c.barrier()
            if "hpre" in tap_outs:
                c.dma(Q_LOAD, tap_outs["hpre"], hpre_d, dS); c.qjoin(Q_LOAD); c.barrier()

            ckstop(11)
            global_scalar(hmax_all[:, 0:IC], IC, cc2i, cc2o, 1, sh_vec, rsh_q, 2)
            tv = c.op("vector", lambda e: e.tensor_scalar(
                sdh_bc[:, :], sdh_bc[:, :], sh_vec[:, 0:1], None, ALU.mult))
            c.barrier()

            ckstop(12)
            hqfree = {}
            tvq = None
            ch23 = CH[:, 2 * D:4 * D]
            for ic in range(IC):
                hbf = CHbf[ic % 2][:, 0:T]
                ssem, stgt = xs.fill(hbf, hpre_d[ic], free=hqfree.get(ic - 2))
                c.wait("vector", ssem, stgt)
                tv1 = c.op("vector", lambda e, hb=hbf: e.tensor_scalar(
                    ch23[:, 0:T], hb, rsh_q[:, 0:1], RND, ALU.mult, ALU.add))
                hqfree[ic] = ("p_vector", tv1)
                c.op("vector", lambda e: e.tensor_scalar(
                    ch23[:, 0:T], ch23[:, 0:T], -RND, QMAX, ALU.add, ALU.min), inc=False)
                tvq = c.op("vector", lambda e, icc=ic: e.tensor_scalar(
                    hq_row(icc), ch23[:, 0:T], -(QMAX + 1.0), None, ALU.max))
            c.wp("tensor", "vector", tvq)
            dev = {}
            oprev = {}
            xbfree = {-2: hqfree.get(IC - 2), -1: hqfree.get(IC - 1)}
            for tc in range(TC):
                bank2 = psT2[tc % 2]
                if tc >= 2:
                    c.wp("tensor", "vector", dev[tc - 2])
                last = None
                for ic in range(IC):
                    for nsp in range(2):
                        def mmf(e, icc=ic, nspp=nsp, tcc=tc, bank2=bank2):
                            return e.matmul(bank2[:, nspp * 512:(nspp + 1) * 512],
                                            hq_tile(icc, tcc),
                                            qwdT[:, icc, nspp * 512:(nspp + 1) * 512],
                                            start=(icc == 0), stop=(icc == IC - 1))
                        last = c.op("tensor", mmf, inc=(ic == IC - 1 and nsp == 1))
                xbuf = CHs[tc % 2]
                ssem, stgt = xs.fill(xbuf, xmid_d[tc * 128:(tc + 1) * 128, :], free=xbfree.get(tc - 2))
                c.wait("vector", ssem, stgt)
                c.wp("vector", "tensor", last)
                if tc >= 1:
                    sem, tgt = oprev[tc - 1]
                    c.wait("vector", sem, tgt)
                c.op("vector", lambda e, bank2=bank2: e.tensor_tensor(
                    CHs[2][:, :], bank2[:, 0:1024], sdh_bc[:, :], ALU.mult), inc=False)
                tv = c.op("vector", lambda e, xb=xbuf: e.tensor_tensor(
                    CHs[3][:, :], CHs[2][:, :], xb, ALU.add))
                dev[tc] = tv
                xbfree[tc] = ("p_vector", tv)
                oprev[tc] = o_st.store(out_h[tc * 128:(tc + 1) * 128, :], CHs[3][:, :], "vector", tv)
            c.wait(Q_STORE, o_st.sl[0], c.cnt[o_st.sl[0]])
            c.barrier()


        except _Stop:
            c.barrier()

        with nc.Block() as block:
            @block.tensor
            def _(e):
                for f in c.ops["tensor"]:
                    f(e)

            @block.vector
            def _(e):
                for f in c.ops["vector"]:
                    f(e)

            @block.scalar
            def _(e):
                for f in c.ops["scalar"]:
                    f(e)

            @block.gpsimd
            def _(e):
                for f in c.ops["gpsimd"]:
                    f(e)

            @block.sync
            def _(e):
                for f in c.ops["sync"]:
                    f(e)

    return nc




def kernel(**inputs):
    X = np.ascontiguousarray(np.asarray(inputs["L0_attn_x"], dtype=np.float32))
    B, T, D_ = X.shape
    nc = build_program(T=T)
    common = {
        "ln_g": inputs["L0_attn_ln_g"], "ln_b": inputs["L0_attn_ln_b"],
        "h1w1": inputs["L0_attn_h1_w1"], "h1b1": inputs["L0_attn_h1_b1"],
        "h1w2": inputs["L0_attn_h1_w2"], "h1b2": inputs["L0_attn_h1_b2"],
        "h2w1": inputs["L0_attn_h2_w1"], "h2b1": inputs["L0_attn_h2_b1"],
        "h2w2": inputs["L0_attn_h2_w2"], "h2b2": inputs["L0_attn_h2_b2"],
        "fmw1": inputs["L0_attn_fm_w1"], "fmb1": inputs["L0_attn_fm_b1"],
        "fmw2": inputs["L0_attn_fm_w2"], "fmb2": inputs["L0_attn_fm_b2"],
        "clip_x": inputs["L0_ffn_clip_x"], "clip_down": inputs["L0_ffn_clip_down"],
        "w_gate": inputs["L0_ffn_w_gate"], "w_up": inputs["L0_ffn_w_up"],
        "w_down": inputs["L0_ffn_w_down"],
    }
    common = {k: np.ascontiguousarray(np.asarray(v, dtype=np.float32)) for k, v in common.items()}
    in_maps = [dict(common, x=np.ascontiguousarray(X[b])) for b in range(B)]
    res = run_bass_kernel_spmd(nc, in_maps, core_ids=list(range(8)))
    return np.stack([res.results[b]["out"] for b in range(B)], axis=0)
